# version 23
# baseline (speedup 1.0000x reference)
import contextlib
import numpy as np

import concourse.bass as bass
import concourse.mybir as mybir
from concourse.bass_utils import run_bass_kernel_spmd

F32 = mybir.dt.float32
BF16 = mybir.dt.bfloat16
U8 = mybir.dt.uint8
AF = mybir.ActivationFunctionType
ALU = mybir.AluOpType
AX = mybir.AxisListType

NCORES = 8
NT = 8
H = 512
SIGK = 6.25e-4
EPS_LN = 1e-5

RS0, RS1, RS2 = 0, 32, 64
B_BQ1, B_BK1, B_BK2, B_BPROJ = 0, 1, 2, 3
B_BEXP = 4

C_SUM, C_MEAN, C_VARS, C_VEPS, C_LNV, C_RSTD = 0, 1, 2, 3, 4, 5
C_ESUM4, C_ESUM, C_ERCP = 6, 14, 15

OV_BYTES = 180224
USE_WEXPB = True
USE_WARMUP = False
USE_EARLY_PHI = False
USE_EARLY_HT = False
OFF_WCAT = 0
OFF_WPROJ = 49152
OFF_ZQ = 65536
OFF_ZP2 = 81920
OFF_K2 = 98304
OFF_K = 114688
OFF_KV = 131072
OFF_H = 147456
OFF_PHI = 0
OFF_FSTG = 32768
OFF_COMBT = 49152
OFF_D0 = 114688
OFF_WEXP = 114688
OFF_Y = 114688


class EQ:
    ENGS = ("T", "V", "S", "G", "Y")

    def __init__(self, nc, ctx):
        self.nc = nc
        self.ctx = ctx
        self.q = {e: [] for e in self.ENGS}
        self.sems = {}
        self.cnt = {}
        self.last_wait = {e: {} for e in self.ENGS}
        for e in self.ENGS:
            self._mk(e)

    def _mk(self, name):
        if name not in self.sems:
            self.sems[name] = self.ctx.enter_context(self.nc.semaphore("sem_" + name))
            self.cnt[name] = 0

    def op(self, e, fn, inc=1, key=None):
        key = key or e
        self._mk(key)
        sem = self.sems[key]
        self.cnt[key] += inc
        val = self.cnt[key]
        selfw = (e in ("V", "S") and inc == 1 and key == e)
        def run(eng, fn=fn, sem=sem, inc=inc, val=val, selfw=selfw):
            fn(eng).then_inc(sem, inc)
            if selfw:
                eng.wait_ge(sem, val)
        self.q[e].append(run)
        return (key, val)

    def silent(self, e, fn):
        self.q[e].append(fn)

    def wait(self, e, mark):
        if mark is None:
            return
        key, val = mark
        if val is None or val <= 0:
            return
        if self.last_wait[e].get(key, 0) >= val:
            return
        self.last_wait[e][key] = val
        sem = self.sems[key]
        self.q[e].append(lambda eng, sem=sem, val=val: eng.wait_ge(sem, val))

    def mark(self, key):
        self._mk(key)
        return (key, self.cnt[key])


def _bv(ov, off, nbytes, dtype):
    return ov.ap()[:, off:off + nbytes].bitcast(dtype)


def build_nc(upto="full", debug=False):
    nc = bass.Bass("TRN2", target_bir_lowering=False)

    x_d = nc.dram_tensor("x", [2, 1025, 1024], F32, kind="ExternalInput").ap()
    wcat_d = nc.dram_tensor("wcat", [1024, 3072], F32, kind="ExternalInput").ap()
    wproj_d = nc.dram_tensor("wproj", [1024, 1024], F32, kind="ExternalInput").ap()
    phi_d = nc.dram_tensor("phi", [1024, 2048], F32, kind="ExternalInput").ap()
    wexp_d = nc.dram_tensor("wexp", [8, 1024, 1024], F32, kind="ExternalInput").ap()
    brows_d = nc.dram_tensor("brows", [12, 1024], F32, kind="ExternalInput").ap()
    tri_d = nc.dram_tensor("tri", [128, 256], F32, kind="ExternalInput").ap()
    pmask_d = nc.dram_tensor("pmask", [96, 132], F32, kind="ExternalInput").ap()
    out_d = nc.dram_tensor("out", [2, 1024, 1024], F32, kind="ExternalOutput").ap()

    h32_d = nc.dram_tensor("h32", [2048, 1024], F32, kind="Internal").ap()
    zp2_d = nc.dram_tensor("zp2b", [1025, 1024], BF16, kind="Internal").ap()
    q_d = nc.dram_tensor("qb", [1025, 1024], BF16, kind="Internal").ap()
    oar_d = nc.dram_tensor("oarb", [1025, 1024], BF16, kind="Internal").ap()
    ps_d = nc.dram_tensor("ps", [64, 2, 32, 1024], BF16, kind="Internal")
    psred_d = nc.dram_tensor("psred", [8, 2, 32, 1024], BF16, kind="Internal")
    yin_d = nc.dram_tensor("yin", [512, 1024], BF16, kind="Internal")
    yag_d = nc.dram_tensor("yag", [4096, 1024], BF16, kind="Internal")
    wexpb_d = nc.dram_tensor("wexpb", [8, 1024, 1024], BF16, kind="Internal").ap()
    wu_in = nc.dram_tensor("wu_in", [1, 128], F32, kind="Internal")
    wu_out = nc.dram_tensor("wu_out", [8, 128], F32, kind="Internal")
    agt_in = [nc.dram_tensor("agt_in%d" % b, [2, 1024], F32, kind="Internal") for b in range(2)]
    agt_out = [nc.dram_tensor("agt_out%d" % b, [16, 1024], F32, kind="Internal") for b in range(2)]
    ags_in = [nc.dram_tensor("ags_in%d" % b, [1, 1024], F32, kind="Internal") for b in range(2)]
    ags_out = [nc.dram_tensor("ags_out%d" % b, [8, 1024], F32, kind="Internal") for b in range(2)]

    dbg = {}
    if debug:
        for name, shape in [("dbg_h", [2048, 1024]), ("dbg_rows", [96, 1024]),
                            ("dbg_q", [1024, 2048]), ("dbg_oar", [1024, 2048]),
                            ("dbg_d0", [1024, 2048]), ("dbg_slots", [128, 2048])]:
            dbg[name] = nc.dram_tensor(name, shape, F32, kind="ExternalOutput").ap()

    ctx = contextlib.ExitStack()
    sb = lambda n, s, d: ctx.enter_context(nc.sbuf_tensor(n, s, d))

    ov = ctx.enter_context(nc.sbuf_tensor("ov", [128, OV_BYTES], U8))
    rows = sb("rows_s", [96, 1024], F32).ap()
    pmask2 = sb("pmask2_s", [96, 132], F32).ap()
    pmask = pmask2[0:16, 0:4]
    onesr = pmask2[:, 4:132]
    onesrb = sb("onesrb_s", [96, 128], BF16).ap()
    stats = sb("stats", [128, 18], F32).ap()
    tri2 = sb("tri_s", [128, 256], BF16).ap()
    tri_sb = tri2[:, 0:128]
    allon = tri2[:, 128:256]
    onec_sb = allon[:, 0:1]
    x_stage = sb("x_stage", [128, 2048], F32).ap().rearrange("p (u d) -> p u d", u=2)
    xb2 = sb("xb_stage", [128, 2048], BF16).ap().rearrange("p (u d) -> p u d", u=2)
    xt_stage2 = sb("xt_stage", [128, 2, 1024], BF16).ap()
    xt_stage = xt_stage2[:, 0, :]
    sq = sb("sq", [128, 1024], F32).ap()
    h32_stage = sb("h32_stage", [128, 1024], F32).ap()
    haloT = sb("haloT", [128, 256], BF16).ap().rearrange("p (b c r) -> p b c r", b=2, c=8)

    wcat = _bv(ov, OFF_WCAT, 49152, BF16).rearrange("p (c o) -> p c o", c=8)
    wproj = _bv(ov, OFF_WPROJ, 16384, BF16).rearrange("p (c o) -> p c o", c=8)
    zq = _bv(ov, OFF_ZQ, 16384, BF16).rearrange("p (j d) -> p j d", j=8)
    zp2 = _bv(ov, OFF_ZP2, 16384, BF16).rearrange("p (j d) -> p j d", j=8)
    k2 = _bv(ov, OFF_K2, 16384, BF16).rearrange("p (j d) -> p j d", j=8)
    kk = _bv(ov, OFF_K, 16384, BF16).rearrange("p (j d) -> p j d", j=8)
    kv = _bv(ov, OFF_KV, 16384, BF16).rearrange("p (j d) -> p j d", j=8)
    hh = _bv(ov, OFF_H, 32768, BF16).rearrange("p (t d) -> p t d", t=16)
    phi = _bv(ov, OFF_PHI, 32768, BF16).rearrange("p (c s) -> p c s", c=8)
    combT = _bv(ov, OFF_COMBT, 65536, BF16).rearrange("p (st b j m) -> p st b j m", st=16, b=2, j=8)
    d0 = _bv(ov, OFF_D0, 32768, BF16).rearrange("p (j s) -> p j s", j=8)
    wexp = _bv(ov, OFF_WEXP, 32768, BF16).rearrange("p (u c o) -> p u c o", u=2, c=8)
    yy = _bv(ov, OFF_Y, 65536, BF16).rearrange("p (st b e) -> p st b e", st=16, b=2)
    ht_stage = _bv(ov, OFF_FSTG, 4096, BF16).rearrange("p (u d) -> p u d", u=2)
    l0_sb = _bv(ov, OFF_FSTG + 4096, 8192, F32)
    e2_sb = _bv(ov, OFF_FSTG + 12288, 4096, BF16)
    e2x = _bv(ov, OFF_FSTG + 4096, 8192, BF16).rearrange("p (u d) -> p u d", u=2)
    rcpf = _bv(ov, OFF_FSTG + 12288, 4096, F32)
    rcpb = sq.bitcast(BF16)
    sl_stage = _bv(ov, OFF_PHI, 4096, BF16).rearrange("p (u d) -> p u d", u=2)
    d1_stage = _bv(ov, OFF_PHI + 8192, 2048, BF16)
    sc_stage = _bv(ov, OFF_PHI + 10240, 8192, F32).rearrange("p (u d) -> p u d", u=2)
    scb_stage = _bv(ov, OFF_PHI + 18432, 4096, BF16).rearrange("p (u d) -> p u d", u=2)
    sct_stage = _bv(ov, OFF_PHI + 22528, 4096, BF16).rearrange("p (u d) -> p u d", u=2)
    y_stage = _bv(ov, OFF_PHI + 26624, 4096, BF16).rearrange("p (u d) -> p u d", u=2)

    pb = [ctx.enter_context(nc.psum_tensor("pb%d" % i, [128, H], F32)) for i in range(8)]

    eq = EQ(nc, ctx)
    T, V, S, G, Y = "T", "V", "S", "G", "Y"
    def ones_at(base, n=128):
        return onesr[base:base + 1, 0:n]

    def onesb_at(base, n=128):
        return onesrb[base:base + 1, 0:n]

    rowsb = rows.bitcast(BF16)

    def biasrow(base, half):
        return rowsb[base:base + 1, half * H:(half + 1) * H]

    sqb = sq.bitcast(BF16)
    rowz_qh = sqb[0:1, 1024:2048]
    rowz_oarh = sqb[0:1, 0:1024]
    rowz_sh = rowz_oarh
    agst = rows[0:16, :]
    bank_free = [None] * 8

    def t_acquire(i):
        eq.wait(T, bank_free[i])

    def hs(half):
        return slice(half * H, (half + 1) * H)

    xbuf_free = [None, None]
    xb_cons = [[], []]

    def xb_wait(buf, eng=S):
        for mk in xb_cons[buf]:
            eq.wait(eng, mk)
        del xb_cons[buf][:]

    def load_x_tile(b, j, buf):
        eq.wait(Y, xbuf_free[buf])
        return eq.op(Y, lambda y, b=b, j=j, buf=buf: y.dma_start(
            out=x_stage[:, buf, :], in_=x_d[b, 1 + j * 128:1 + (j + 1) * 128, :]),
            inc=16, key="yx%d" % buf)

    if USE_WARMUP:
        m_wu = eq.op(G, lambda g: g.dma_start(out=wu_in.ap(), in_=pmask_d[0:1, 0:128]), inc=16, key="gwc")
        eq.wait(G, m_wu)
        eq.op(G, lambda g: g.collective_compute(
            "AllGather", ALU.bypass, replica_groups=[list(range(NCORES))],
            ins=[wu_in.ap().opt()], outs=[wu_out.ap().opt()]), key="gcc")
    eq.op(G, lambda g: g.dma_start(out=onesrb, in_=pmask_d[:, 4:132]), inc=16, key="gw")
    m_tri = eq.op(G, lambda g: g.dma_start(out=tri2, in_=tri_d), inc=16, key="gw")
    m_onec = m_tri
    m_pmask = eq.op(G, lambda g: g.dma_start(out=pmask2, in_=pmask_d), inc=16, key="gw")
    m_wcat = eq.op(G, lambda g: g.dma_start(
        out=wcat, in_=wcat_d.rearrange("(c p) o -> p c o", p=128)), inc=16, key="gw")
    m_wproj = eq.op(G, lambda g: g.dma_start(
        out=wproj, in_=wproj_d.rearrange("(c p) o -> p c o", p=128)), inc=16, key="gw")
    m_gw0 = eq.mark("gw")
    m_rows = m_onesr = m_tri = m_onec = m_pmask = m_wcat = m_wproj = m_gw0

    m_ms = eq.op(V, lambda v: v.memzero(xb2[:, 0, :]))
    eq.wait(G, m_ms)
    m_hl = [eq.op(G, lambda g, b=b: g.dma_start(
        out=xb2[b * 32:b * 32 + 1, 0, :], in_=x_d[b, 0:1, :]), inc=16, key="ghl") for b in range(2)]
    eq.wait(Y, m_hl[1])
    m_htp = [eq.op(Y, lambda y, b=b: y.dma_start(
        out=haloT[:, b, :, :],
        in_=xb2[b * 32:b * 32 + 16, 0, :].rearrange("p (c f) -> p c f", c=8),
        transpose=True), inc=16, key="yt") for b in range(2)]
    xb_cons[0].append(m_htp[1])

    def emit_bias_slots(b):
        eq.wait(G, eq.mark(T))
        m = []
        for bsrc, rdst in ((B_BQ1, RS0), (B_BK1, RS1), (B_BK2, RS2)):
            m.append(eq.op(G, lambda g, bsrc=bsrc, rdst=rdst: g.dma_start(
                out=rowsb[rdst:rdst + 1, 0:1024], in_=brows_d[bsrc:bsrc + 1, :]), inc=16, key="grow"))
        return m[-1]

    def emit_halo_proj(b, m_bias):
        eq.wait(T, m_htp[1])
        eq.wait(T, m_wcat)
        eq.wait(T, m_bias)
        for pi, (o_base, r_b) in ((0, (0, RS0)), (1, (2048, RS2))):
            for half in range(2):
                bi = 4 + 2 * pi + half
                t_acquire(bi)
                for c in range(8):
                    eq.silent(T, lambda t, b=b, c=c, o=o_base, half=half, bi=bi: t.matmul(
                        pb[bi][0:1, :], haloT[:, b, c, 0:1],
                        wcat[:, c, o + half * H:o + half * H + H],
                        start=(c == 0), stop=False, skip_group_check=True))
                eq.op(T, lambda t, r_b=r_b, half=half, bi=bi: t.matmul(
                    pb[bi][0:1, :], onesb_at(r_b, 1), biasrow(r_b, half),
                    start=False, stop=True, skip_group_check=True))
        m_mm = eq.mark(T)
        eq.wait(V, m_mm)
        for half in range(2):
            eq.op(V, lambda v, half=half: v.tensor_copy(rowz_qh[:, hs(half)], pb[4 + half][0:1, :]))
            eq.op(V, lambda v, half=half: v.tensor_copy(k2[0:1, 7, hs(half)], pb[6 + half][0:1, :]))
        m_e = eq.mark(V)
        for i in range(4, 8):
            bank_free[i] = m_e
        return m_e

    def emit_pass1a(b, m_hprev):
        for i in range(4, 8):
            t_acquire(i)
        eq.wait(V, m_hprev)
        xm = [load_x_tile(b, 0, 0)]
        m_cast = [None] * NT
        m_xt = [None] * NT
        m_kvv = [None] * NT
        m_projlast = None
        m_proj2last = None
        for j in range(NT):
            buf = j % 2
            eq.wait(V, xm[j])
            xb_wait(buf, V)
            m_cast[j] = eq.op(V, lambda v, buf=buf: v.tensor_copy(xb2[:, buf, :], x_stage[:, buf, :]))
            if j + 1 < NT:
                xbuf_free[buf] = m_cast[j]
                xm.append(load_x_tile(b, j + 1, 1 - buf))
            eq.wait(Y, m_cast[j])
            if j >= 2:
                eq.wait(Y, m_proj2last)
            xu = j % 2
            m_xt[j] = eq.op(Y, lambda y, xu=xu, buf=buf: y.dma_start(
                out=xt_stage2[:, xu, :].rearrange("p (t c) -> p t c", t=8),
                in_=xb2[:, buf, :].rearrange("p (t c) -> p t c", t=8),
                transpose=True), inc=16, key="yt%d" % xu)
            xb_cons[buf].append(m_xt[j])
            eq.wait(T, m_xt[j])
            for pi in range(3):
                for half in range(2):
                    bi = (2 * pi + half) % 4
                    t_acquire(bi)
                    o0 = pi * 1024 + half * H
                    for c in range(8):
                        eq.silent(T, lambda t, c=c, o0=o0, bi=bi, xu=xu: t.matmul(
                            pb[bi][:, :], xt_stage2[:, xu, c * 128:(c + 1) * 128],
                            wcat[:, c, o0:o0 + H], start=(c == 0), stop=False,
                            skip_group_check=True))
                    r_b = (RS0, RS1, RS2)[pi]
                    m_mm = eq.op(T, lambda t, r_b=r_b, half=half, bi=bi: t.matmul(
                        pb[bi][:, :], onesb_at(r_b), biasrow(r_b, half),
                        start=False, stop=True, skip_group_check=True))
                    eq.wait(S, m_mm)
                    if pi == 0:
                        fn = lambda s, j=j, half=half, bi=bi: s.copy(zq[:, j, hs(half)], pb[bi][:, :])
                    elif pi == 1:
                        fn = lambda s, j=j, half=half, bi=bi: s.activation(
                            kk[:, j, hs(half)], pb[bi][:, :], AF.Exp)
                    else:
                        fn = lambda s, j=j, half=half, bi=bi: s.copy(zp2[:, j, hs(half)], pb[bi][:, :])
                    m_ev = eq.op(S, fn)
                    bank_free[bi] = m_ev
                    if pi == 2 and half == 1:
                        eq.wait(G, m_ev)
                        eq.op(G, lambda g, j=j: g.dma_start(
                            out=zp2_d[1 + j * 128:1 + (j + 1) * 128, :], in_=zp2[:, j, :]),
                            inc=16, key="gsh")
                if pi == 1:
                    eq.wait(V, eq.mark(S))
                    m_kvv[j] = eq.op(V, lambda v, j=j, buf=buf: v.tensor_mul(kv[:, j, :], kk[:, j, :], xb2[:, buf, :]))
                    xb_cons[buf].append(m_kvv[j])
            m_proj2last = m_projlast
            m_projlast = eq.mark(T)
        eq.wait(T, m_onec)
        for j in range(NT):
            eq.wait(T, m_kvv[j])
            for half in range(2):
                eq.op(T, lambda t, j=j, half=half: t.matmul(
                    pb[4 + half][0:1, :], onec_sb, kk[:, j, hs(half)],
                    start=(j == 0), stop=(j == NT - 1), skip_group_check=True))
                eq.op(T, lambda t, j=j, half=half: t.matmul(
                    pb[6 + half][0:1, :], onec_sb, kv[:, j, hs(half)],
                    start=(j == 0), stop=(j == NT - 1), skip_group_check=True))
        m_tot = eq.mark(T)
        eq.wait(V, m_tot)
        for half in range(2):
            eq.op(V, lambda v, half=half: v.tensor_copy(rows[0:1, hs(half)], pb[4 + half][0:1, :]))
            eq.op(V, lambda v, half=half: v.tensor_copy(h32_stage[0:1, hs(half)], pb[6 + half][0:1, :]))
        m_ev = eq.mark(V)
        for i in range(4, 8):
            bank_free[i] = m_ev
        eq.wait(G, m_ev)
        eq.op(G, lambda g, b=b: g.dma_start(out=agt_in[b].ap()[0:1, :], in_=rows[0:1, :]), inc=16, key="gag")
        eq.op(G, lambda g, b=b: g.dma_start(out=agt_in[b].ap()[1:2, :], in_=h32_stage[0:1, :]), inc=16, key="gag")
        eq.wait(G, eq.mark("gag"))
        eq.op(G, lambda g, b=b: g.collective_compute(
            "AllGather", ALU.bypass, replica_groups=[list(range(NCORES))],
            ins=[agt_in[b].ap().opt()], outs=[agt_out[b].ap().opt()]), key="gcc")
        eq.wait(G, eq.mark("gcc"))
        m_rd = eq.op(G, lambda g, b=b: g.dma_start(out=agst, in_=agt_out[b].ap()), inc=16, key="gag")
        return m_rd

    def emit_shift_zp2(b):
        eq.wait(G, eq.mark(V))
        eq.op(G, lambda g: g.dma_start(out=zp2_d[0:1, :], in_=k2[0:1, 7, :]), inc=16, key="gsh")
        eq.wait(G, eq.mark("gsh"))
        m = eq.op(G, lambda g: g.dma_start(
            out=k2, in_=zp2_d[0:1024, :].rearrange("(j p) d -> p j d", p=128)), inc=16, key="gsh")
        return m

    def emit_sigmoid_batch(b, m_k2rd):
        eq.wait(S, eq.mark(V))
        eq.op(S, lambda s: s.activation(rowz_qh, rowz_qh, AF.Sigmoid))
        for j in range(NT):
            eq.op(S, lambda s, j=j: s.activation(zq[:, j, :], zq[:, j, :], AF.Sigmoid))
        m_sig = eq.mark(S)
        eq.wait(S, m_k2rd)
        for j in range(NT):
            eq.op(S, lambda s, j=j: s.activation(k2[:, j, :], k2[:, j, :], AF.Sigmoid, scale=SIGK))
        m_sigk2 = eq.mark(S)
        eq.wait(G, m_k2rd)
        eq.wait(G, m_sig)
        eq.op(G, lambda g: g.dma_start(
            out=q_d[1:1025, :].rearrange("(j p) d -> p j d", p=128), in_=zq), inc=16, key="gsh")
        m_qw = eq.op(G, lambda g: g.dma_start(out=q_d[0:1, :], in_=rowz_qh), inc=16, key="gsh")
        return m_sig, m_sigk2, m_qw

    def emit_carry_setup(b, m_ag, m_sig):
        eq.wait(T, m_ag)
        eq.wait(T, m_pmask)
        for i in (4, 5, 6, 7):
            t_acquire(i)
        for half in range(2):
            eq.op(T, lambda t, half=half: t.matmul(
                pb[4 + half][0:1, :], pmask[0:16, 0:1], agst[:, hs(half)],
                start=True, stop=True, skip_group_check=True))
            eq.op(T, lambda t, half=half: t.matmul(
                pb[6 + half][0:1, :], pmask[0:16, 1:2], agst[:, hs(half)],
                start=True, stop=True, skip_group_check=True))
        m_mm = eq.mark(T)
        eq.wait(V, m_mm)
        for half in range(2):
            eq.op(V, lambda v, half=half: v.tensor_scalar_add(
                rows[RS0:RS0 + 1, hs(half)], pb[4 + half][0:1, :], 1e-6))
            eq.op(V, lambda v, half=half: v.tensor_copy(h32_stage[0:1, hs(half)], pb[6 + half][0:1, :]))
        m_gkv = eq.mark(V)
        eq.wait(G, m_gkv)
        m_ckv = eq.op(G, lambda g: g.dma_start(out=rows[RS1:RS1 + 1, :], in_=h32_stage[0:1, :]), inc=16, key="grow")
        eq.wait(V, m_ckv)
        eq.op(V, lambda v: v.reciprocal(h32_stage[0:1, :], rows[RS0:RS0 + 1, :]))
        eq.wait(V, eq.mark("V"))
        for half in range(2):
            eq.op(V, lambda v, half=half: v.tensor_mul(h32_stage[0:1, hs(half)], pb[6 + half][0:1, :], h32_stage[0:1, hs(half)]))
        m_sh = eq.op(V, lambda v: v.tensor_copy(rowz_sh, h32_stage[0:1, :]))
        bank_free[4] = bank_free[5] = bank_free[6] = bank_free[7] = m_sh
        eq.wait(V, m_sig)
        m_oarh = eq.op(V, lambda v: v.tensor_mul(rowz_oarh, rowz_oarh, rowz_qh))
        return m_oarh, m_ckv

    def emit_pass1b(b, m_carry, m_ckv, m_sig, m_hprev):
        eq.wait(V, m_hprev)
        eq.wait(T, m_tri)
        eq.wait(T, m_carry)
        eq.wait(T, m_ckv)
        eq.wait(V, m_sig)
        for j in range(NT):
            for qi in range(2):
                srcv = kk if qi == 0 else kv
                carry_r = RS0 if qi == 0 else RS1
                base = 4 * (j % 2)
                b0, b1 = (base, base + 1) if qi == 0 else (base + 2, base + 3)
                t_acquire(b0)
                t_acquire(b1)
                for half, bi in ((0, b0), (1, b1)):
                    eq.silent(T, lambda t, j=j, half=half, srcv=srcv, bi=bi: t.matmul(
                        pb[bi][:, :], tri_sb, srcv[:, j, hs(half)],
                        start=True, stop=False, skip_group_check=True))
                    for i in range(j):
                        eq.silent(T, lambda t, i=i, half=half, srcv=srcv, bi=bi: t.matmul(
                            pb[bi][:, :], allon, srcv[:, i, hs(half)],
                            start=False, stop=False, skip_group_check=True))
                for half, bi in ((0, b0), (1, b1)):
                    eq.op(T, lambda t, half=half, carry_r=carry_r, bi=bi: t.matmul(
                        pb[bi][:, :], ones_at(carry_r), rows[carry_r:carry_r + 1, hs(half)],
                        start=False, stop=True, skip_group_check=True))
                m_mm = eq.mark(T)
                eq.wait(V, m_mm)
                if qi == 0:
                    for half, bi in ((0, b0), (1, b1)):
                        eq.op(V, lambda v, half=half, bi=bi: v.reciprocal(
                            h32_stage[:, hs(half)], pb[bi][:, :]))
                    mk = eq.mark(V)
                    bank_free[b0] = bank_free[b1] = mk
                else:
                    eq.wait(V, eq.mark("V"))
                    for half, bi in ((0, b0), (1, b1)):
                        eq.op(V, lambda v, half=half, bi=bi: v.tensor_mul(
                            xt_stage[:, hs(half)], pb[bi][:, :], h32_stage[:, hs(half)]))
                    mk = eq.mark(V)
                    bank_free[b0] = bank_free[b1] = mk
                    eq.op(V, lambda v, j=j: v.tensor_mul(zp2[:, j, :], xt_stage, zq[:, j, :]))
        return eq.mark(V)

    def emit_shift_oar_q(b, m_p1b):
        eq.wait(G, m_p1b)
        eq.op(G, lambda g: g.dma_start(
            out=oar_d[1:1025, :].rearrange("(j p) d -> p j d", p=128), in_=zp2), inc=16, key="gsh")
        eq.op(G, lambda g: g.dma_start(out=oar_d[0:1, :], in_=rowz_oarh), inc=16, key="gsh")
        eq.wait(G, eq.mark("gsh"))
        m_oar = eq.op(G, lambda g: g.dma_start(
            out=kk, in_=oar_d[0:1024, :].rearrange("(j p) d -> p j d", p=128)), inc=16, key="gsh")
        m_q = eq.op(G, lambda g: g.dma_start(
            out=zq, in_=q_d[0:1024, :].rearrange("(j p) d -> p j d", p=128)), inc=16, key="gsh")
        return m_q, m_q

    def emit_pass2a(b, m_oarsh, m_sigk2):
        t_acquire(6)
        t_acquire(7)
        eq.wait(V, m_oarsh)
        eq.wait(V, m_sigk2)
        xm = [load_x_tile(b, 0, 0)]
        m_k2e = [None] * NT
        m_cast = [None] * NT
        for j in range(NT):
            buf = j % 2
            eq.wait(S, xm[j])
            xb_wait(buf)
            m_cast[j] = eq.op(S, lambda s, buf=buf: s.copy(xb2[:, buf, :], x_stage[:, buf, :]))
            if j + 1 < NT:
                xbuf_free[buf] = m_cast[j]
                xm.append(load_x_tile(b, j + 1, 1 - buf))
            eq.wait(V, m_cast[j])
            eq.op(V, lambda v, j=j, buf=buf: v.tensor_sub(xb2[:, buf, :], xb2[:, buf, :], kk[:, j, :]))
            if j == 0:
                eq.op(V, lambda v: v.tensor_scalar_mul(xb2[0:1, 0, :], xb2[0:1, 0, :], pmask[0:1, 3:4]))
            m_k2e[j] = eq.op(V, lambda v, j=j, buf=buf: v.tensor_mul(kv[:, j, :], k2[:, j, :], xb2[:, buf, :]))
            xb_cons[buf].append(m_k2e[j])
            eq.wait(T, m_k2e[j])
            for half in range(2):
                eq.op(T, lambda t, j=j, half=half: t.matmul(
                    pb[6 + half][0:1, :], onec_sb, kv[:, j, hs(half)],
                    start=(j == 0), stop=(j == NT - 1), skip_group_check=True))
        xbuf_free[1] = m_cast[NT - 1]
        m_tot = eq.mark(T)
        eq.wait(V, m_tot)
        for half in range(2):
            eq.op(V, lambda v, half=half: v.tensor_copy(rows[0:1, hs(half)], pb[6 + half][0:1, :]))
        m_ev = eq.mark(V)
        bank_free[6] = bank_free[7] = m_ev
        eq.wait(G, m_ev)
        eq.op(G, lambda g, b=b: g.dma_start(out=ags_in[b].ap(), in_=rows[0:1, :]), inc=16, key="gag")
        eq.wait(G, eq.mark("gag"))
        eq.op(G, lambda g, b=b: g.collective_compute(
            "AllGather", ALU.bypass, replica_groups=[list(range(NCORES))],
            ins=[ags_in[b].ap().opt()], outs=[ags_out[b].ap().opt()]), key="gcc")
        eq.wait(G, eq.mark("gcc"))
        m_rd = eq.op(G, lambda g, b=b: g.dma_start(out=agst[0:8, :], in_=ags_out[b].ap()), inc=16, key="gag")
        return m_rd

    def emit_pass2b(b, m_ag, m_qsh):
        eq.wait(T, m_ag)
        t_acquire(4)
        t_acquire(5)
        for half in range(2):
            eq.op(T, lambda t, half=half: t.matmul(
                pb[4 + half][0:1, :], pmask[0:8, 2:3], agst[0:8, hs(half)],
                start=True, stop=True, skip_group_check=True))
        m_mm = eq.mark(T)
        eq.wait(V, m_mm)
        for half in range(2):
            eq.op(V, lambda v, half=half: v.tensor_copy(rows[RS0:RS0 + 1, hs(half)], pb[4 + half][0:1, :]))
        m_cs = eq.mark(V)
        bank_free[4] = bank_free[5] = m_cs
        eq.wait(G, eq.mark(T))
        m_bp = eq.op(G, lambda g: g.dma_start(out=rowsb[RS1:RS1 + 1, 0:1024], in_=brows_d[B_BPROJ:B_BPROJ + 1, :]), inc=16, key="grow")
        eq.wait(T, m_cs)
        eq.wait(V, m_qsh)
        m_h32dma = None
        m_at = [None] * NT
        xm = []
        for j in range(NT + 1):
            if j < NT:
                buf = j % 2
                if j == 0:
                    xm.append(load_x_tile(b, 0, 0))
                    xm.append(load_x_tile(b, 1, 1))
                s2a = 4 * (j % 2)
                t_acquire(s2a)
                t_acquire(s2a + 1)
                for half, bi in ((0, s2a), (1, s2a + 1)):
                    eq.silent(T, lambda t, j=j, half=half, bi=bi: t.matmul(
                        pb[bi][:, :], tri_sb, kv[:, j, hs(half)],
                        start=True, stop=False, skip_group_check=True))
                    for i in range(j):
                        eq.silent(T, lambda t, i=i, half=half, bi=bi: t.matmul(
                            pb[bi][:, :], allon, kv[:, i, hs(half)],
                            start=False, stop=False, skip_group_check=True))
                for half, bi in ((0, s2a), (1, s2a + 1)):
                    eq.op(T, lambda t, half=half, bi=bi: t.matmul(
                        pb[bi][:, :], ones_at(RS0), rows[RS0:RS0 + 1, hs(half)],
                        start=False, stop=True, skip_group_check=True))
                m_mm = eq.mark(T)
                eq.op(V, lambda v, j=j: v.tensor_scalar_mul(sq[1:128, :], zq[0:127, j, :], SIGK)
                      if False else v.tensor_scalar_mul(sq, zq[:, j, :], SIGK))
                if j == 0:
                    eq.op(V, lambda v: v.tensor_scalar_mul(sq[0:1, :], sq[0:1, :], pmask[0:1, 3:4]))
                eq.wait(V, m_mm)
                xu = j % 2
                for half, bi in ((0, s2a), (1, s2a + 1)):
                    eq.op(V, lambda v, half=half, bi=bi, xu=xu: v.tensor_mul(
                        xt_stage2[:, xu, hs(half)], pb[bi][:, :], sq[:, hs(half)]))
                mk = eq.mark(V)
                bank_free[s2a] = bank_free[s2a + 1] = mk
                m_a = eq.op(V, lambda v, j=j, xu=xu: v.tensor_add(kk[:, j, :], xt_stage2[:, xu, :], zp2[:, j, :]))
                eq.wait(Y, m_a)
                if j >= 2:
                    eq.wait(Y, m_pj2)
                m_at[j] = eq.op(Y, lambda y, j=j, xu=xu: y.dma_start(
                    out=xt_stage2[:, xu, :].rearrange("p (t c) -> p t c", t=8),
                    in_=kk[:, j, :].rearrange("p (t c) -> p t c", t=8),
                    transpose=True), inc=16, key="yt%d" % xu)
            if j >= 1:
                pj = j - 1
                buf = pj % 2
                xu = pj % 2
                eq.wait(T, m_at[pj])
                eq.wait(T, m_wproj)
                eq.wait(T, m_bp)
                pja = 4 * (pj % 2) + 2
                for half, bi in ((0, pja), (1, pja + 1)):
                    t_acquire(bi)
                    for c in range(8):
                        eq.silent(T, lambda t, c=c, half=half, bi=bi, xu=xu: t.matmul(
                            pb[bi][:, :], xt_stage2[:, xu, c * 128:(c + 1) * 128],
                            wproj[:, c, hs(half)], start=(c == 0), stop=False, skip_group_check=True))
                    eq.op(T, lambda t, half=half, bi=bi: t.matmul(
                        pb[bi][:, :], onesb_at(RS1), biasrow(RS1, half),
                        start=False, stop=True, skip_group_check=True))
                m_pj2 = eq.mark(T)
                eq.wait(V, m_pj2)
                eq.wait(V, xm[pj])
                eq.wait(V, m_h32dma)
                for half, bi in ((0, pja), (1, pja + 1)):
                    eq.op(V, lambda v, half=half, bi=bi, buf=buf: v.tensor_add(
                        h32_stage[:, hs(half)], pb[bi][:, :], x_stage[:, buf, hs(half)]))
                m_r = eq.mark(V)
                bank_free[pja] = bank_free[pja + 1] = m_r
                xbuf_free[buf] = m_r
                if pj + 2 < NT:
                    xm.append(load_x_tile(b, pj + 2, buf))
                eq.wait(S, m_r)
                m_rsq = eq.op(S, lambda s: s.activation(sq, h32_stage, AF.Square,
                                                        accum_out=stats[:, C_VARS:C_VARS + 1]))
                eq.op(V, lambda v: v.reduce_sum(stats[:, C_SUM:C_SUM + 1], h32_stage, axis=AX.X))
                eq.wait(V, eq.mark("V"))
                eq.op(V, lambda v: v.tensor_scalar_mul(stats[:, C_MEAN:C_MEAN + 1], stats[:, C_SUM:C_SUM + 1], 1.0 / 1024))
                eq.wait(V, eq.mark("V"))
                eq.wait(V, m_rsq)
                eq.op(V, lambda v: v.tensor_scalar(
                    stats[:, C_LNV:C_LNV + 1], stats[:, C_MEAN:C_MEAN + 1], stats[:, C_MEAN:C_MEAN + 1], None,
                    ALU.mult))
                eq.wait(V, eq.mark("V"))
                eq.op(V, lambda v: v.tensor_scalar(
                    stats[:, C_VEPS:C_VEPS + 1], stats[:, C_VARS:C_VARS + 1], 1.0 / 1024, EPS_LN, ALU.mult, ALU.add))
                eq.wait(V, eq.mark("V"))
                m_v = eq.op(V, lambda v: v.tensor_sub(
                    stats[:, C_VEPS:C_VEPS + 1], stats[:, C_VEPS:C_VEPS + 1], stats[:, C_LNV:C_LNV + 1]))
                eq.wait(S, m_v)
                eq.op(S, lambda s: s.activation(stats[:, C_LNV:C_LNV + 1], stats[:, C_VEPS:C_VEPS + 1], AF.Ln))
                eq.wait(S, eq.mark("S"))
                m_rs = eq.op(S, lambda s: s.activation(stats[:, C_RSTD:C_RSTD + 1], stats[:, C_LNV:C_LNV + 1],
                                                       AF.Exp, scale=-0.5))
                eq.wait(V, m_rs)
                eq.op(V, lambda v: v.tensor_scalar(
                    h32_stage, h32_stage, stats[:, C_MEAN:C_MEAN + 1],
                    stats[:, C_RSTD:C_RSTD + 1], ALU.subtract, ALU.mult))
                m_h32 = eq.op(V, lambda v, b=b, pj=pj: v.tensor_copy(hh[:, b * 8 + pj, :], h32_stage))
                eq.wait(S, m_h32)
                m_h32dma = eq.op(S, lambda s, b=b, pj=pj: s.dma_start(
                    out=h32_d[b * 1024 + pj * 128:b * 1024 + (pj + 1) * 128, :], in_=h32_stage), inc=16, key="sdh")
        return m_h32dma

    def emit_wexp_cast(n):
        eq.op(G, lambda g, n=n: g.dma_start(out=wexpb_d[n], in_=wexp_d[n]),
              inc=16, key="gwc")

    h_done = []
    m_phi = None
    if not USE_EARLY_PHI:
        pass
    for b in range(2):
        m_bias = emit_bias_slots(b)
        emit_halo_proj(b, m_bias)
        m_ag1 = emit_pass1a(b, h_done[-1] if h_done else None)
        if b == 1:
            m_wcat_T_free = eq.mark(T)
        if USE_WEXPB:
            for n in range(4 * b, 4 * b + 2):
                emit_wexp_cast(n)
        m_k2rd = emit_shift_zp2(b)
        m_sig, m_sigk2, m_qw = emit_sigmoid_batch(b, m_k2rd)
        m_carry, m_ckv = emit_carry_setup(b, m_ag1, m_sig)
        m_p1b = emit_pass1b(b, m_carry, m_ckv, m_sig, h_done[-1] if h_done else None)
        m_oarsh, m_qsh = emit_shift_oar_q(b, m_p1b)
        m_ag2 = emit_pass2a(b, m_oarsh, m_sigk2)
        if USE_WEXPB:
            for n in range(4 * b + 2, 4 * b + 4):
                emit_wexp_cast(n)
        if b == 1 and USE_EARLY_PHI:
            eq.wait(G, m_wcat_T_free)
            m_phi = eq.op(G, lambda g: g.dma_start(
                out=phi, in_=phi_d.rearrange("(c p) s -> p c s", p=128)), inc=16, key="gw")
        m_h = emit_pass2b(b, m_ag2, m_qsh)
        h_done.append(m_h)
    del b
    m_att_T = eq.mark(T)
    m_att_S = eq.mark(S)
    if not USE_EARLY_PHI:
        eq.wait(G, eq.mark(T))
        eq.wait(G, eq.mark(S))
        m_phi = eq.op(G, lambda g: g.dma_start(
            out=phi, in_=phi_d.rearrange("(c p) s -> p c s", p=128)), inc=16, key="gw")

    if upto == "attn":
        _finalize(nc, eq, ctx, dbg, debug, h32_d=h32_d, rows=rows, d0=d0, zq=zq, kk=kk)
        return nc

    m_e2_free = [None, None]
    m_l_prev = [None, None]
    m_d0_prev = None
    eq.wait(V, eq.mark("sdh"))
    for mj in range(NT):
        m_ht = []
        eq.wait(Y, eq.mark("yt"))
        for b in range(2):
            eq.wait(Y, h_done[1])
            eq.wait(Y, m_phi)
            eq.wait(Y, m_l_prev[b])
            m_ht.append(eq.op(Y, lambda y, b=b, mj=mj: y.dma_start(
                out=ht_stage[:, b, :].rearrange("p (t c) -> p t c", t=8),
                in_=hh[:, b * 8 + mj, :].rearrange("p (t c) -> p t c", t=8),
                transpose=True), inc=16, key="yt"))
        eq.wait(T, m_phi)
        m_l = [None, None]
        m_e2 = [None, None]
        for b in range(2):
            eq.wait(T, m_ht[1])
            for ch in range(4):
                bi = 4 * b + ch
                t_acquire(bi)
                for c in range(8):
                    emitfn = eq.op if c == 7 else eq.silent
                    emitfn(T, lambda t, b=b, c=c, ch=ch, bi=bi: t.matmul(
                        pb[bi][:, :], ht_stage[:, b, c * 128:(c + 1) * 128],
                        phi[:, c, ch * H:(ch + 1) * H], start=(c == 0), stop=(c == 7),
                        skip_group_check=True))
            m_l[b] = eq.mark(T)
            eq.wait(S, m_l[b])
            eq.wait(S, m_e2_free[b])
            for ch in range(4):
                eq.op(S, lambda s, b=b, ch=ch: s.activation(
                    e2x[:, b, ch * H:(ch + 1) * H], pb[4 * b + ch][:, :], AF.Exp,
                    accum_out=stats[:, C_ESUM4 + 4 * b + ch:C_ESUM4 + 4 * b + ch + 1]))
            m_e2[b] = eq.mark(S)
            for ch in range(4):
                bank_free[4 * b + ch] = m_e2[b]
        m_l_prev = m_l
        eq.wait(V, m_e2[1])
        eq.wait(V, m_d0_prev)
        m_a0 = eq.op(V, lambda v: v.tensor_add(rcpf, e2x[:, 0, 0:1024], e2x[:, 1, 0:1024]))
        m_a1 = eq.op(V, lambda v: v.tensor_add(h32_stage, e2x[:, 0, 1024:2048], e2x[:, 1, 1024:2048]))
        eq.wait(S, m_a1)
        eq.op(S, lambda s: s.activation(rcpf, rcpf, AF.Ln))
        eq.op(S, lambda s: s.activation(rcpb[:, 0:1024], rcpf, AF.Exp, scale=-1.0))
        eq.op(S, lambda s: s.activation(h32_stage, h32_stage, AF.Ln))
        eq.op(S, lambda s: s.activation(rcpb[:, 1024:2048], h32_stage, AF.Exp, scale=-1.0))
        m_dS = eq.mark(S)
        eq.wait(V, m_dS)
        m_d0_prev = eq.op(V, lambda v, mj=mj: v.tensor_mul(d0[:, mj, :], e2x[:, 0, :], rcpb))
        for b in range(2):
            eq.wait(V, m_e2[b])
            eq.op(V, lambda v, b=b: v.reduce_sum(
                stats[:, C_ESUM:C_ESUM + 1], stats[:, C_ESUM4 + 4 * b:C_ESUM4 + 4 * b + 4], axis=AX.X))
            eq.wait(V, eq.mark("V"))
            eq.op(V, lambda v: v.reciprocal(stats[:, C_ERCP:C_ERCP + 1], stats[:, C_ESUM:C_ESUM + 1]))
            eq.wait(V, eq.mark("V"))
            m_cb = eq.op(V, lambda v, b=b: v.tensor_scalar_mul(e2x[:, b, :], e2x[:, b, :], stats[:, C_ERCP:C_ERCP + 1]))
            eq.wait(S, m_cb)
            m_comb_tp = eq.op(S, lambda s, b=b, mj=mj: s.dma_start(
                out=combT[:, :, b, mj, :],
                in_=e2x[:, b, :].rearrange("p (t c) -> p t c", t=16),
                transpose=True), inc=16, key="st")
            if b == 1:
                m_e2_free = [m_comb_tp, m_comb_tp]
    m_ffni_V = eq.mark(V)
    m_ffni_Y = eq.mark(Y)
    m_ffni_T = eq.mark(T)

    eq.wait(T, m_ffni_V)
    eq.wait(S, m_ffni_V)
    eq.wait(V, m_ffni_T)
    wexp0 = _bv(ov, OFF_FSTG, 16384, BF16).rearrange("p (c o) -> p c o", c=8)
    psred_rows = psred_d.ap().rearrange("n b p d -> (n b p) d")
    sl_dma = []
    m_slots_T_b1 = None
    m_rs = [None, None]
    m_agy = [None, None]
    m_ymm = {}
    m_ydma = {}

    slots_gate = {"b1": None}

    def emit_slots_st(st):
        nonlocal_marks = slots_gate
        if True:
            for b in range(2):
                u = b
                if b == 1:
                    eq.wait(S, nonlocal_marks.get("b1"))
                    for mj in range(NT):
                        eq.op(S, lambda s, st=st, mj=mj: s.activation(
                            d1_stage[:, mj * 128:(mj + 1) * 128], d0[:, mj, st * 128:(st + 1) * 128],
                            AF.Copy, bias=1.0, scale=-1.0))
                m_d1 = eq.mark(S)
                bi0, bi1 = 2 * b, 2 * b + 1
                t_acquire(bi0)
                t_acquire(bi1)
                if b == 1:
                    eq.wait(T, m_d1)
                for mj in range(NT):
                    emitfn = eq.op if mj == NT - 1 else eq.silent
                    for half, bi in ((0, bi0), (1, bi1)):
                        if b == 0:
                            emitfn(T, lambda t, st=st, mj=mj, half=half, bi=bi: t.matmul(
                                pb[bi][:, :], d0[:, mj, st * 128:(st + 1) * 128],
                                hh[:, mj, hs(half)],
                                start=(mj == 0), stop=(mj == NT - 1), skip_group_check=True))
                        else:
                            emitfn(T, lambda t, st=st, mj=mj, half=half, bi=bi: t.matmul(
                                pb[bi][:, :], d1_stage[:, mj * 128:(mj + 1) * 128],
                                hh[:, 8 + mj, hs(half)],
                                start=(mj == 0), stop=(mj == NT - 1), skip_group_check=True))
                m_mm = eq.mark(T)
                if b == 1:
                    nonlocal_marks["b1"] = m_mm
                eq.wait(V, m_mm)
                if len(sl_dma) >= 2:
                    eq.wait(V, sl_dma[-2])
                for half, bi in ((0, bi0), (1, bi1)):
                    eq.op(V, lambda v, u=u, half=half, bi=bi: v.tensor_copy(sl_stage[:, u, hs(half)], pb[bi][:, :]))
                m_ev = eq.mark(V)
                bank_free[bi0] = bank_free[bi1] = m_ev
                eq.wait(G, m_ev)
                sl_dma.append(eq.op(G, lambda g, st=st, b=b, u=u: g.dma_start(
                    out=ps_d.ap()[4 * st:4 * (st + 1), b, :, :], in_=sl_stage[:, u, :]),
                    inc=16, key="gp%d" % u))

    def emit_rs(ch):
        eq.wait(G, eq.mark("gp0"))
        eq.wait(G, eq.mark("gp1"))
        m_rs[ch] = eq.op(G, lambda g, ch=ch: g.collective_compute(
            "ReduceScatter", ALU.add, replica_groups=[list(range(NCORES))],
            ins=[ps_d.ap()[32 * ch:32 * (ch + 1), :, :, :].opt()],
            outs=[psred_d.ap()[4 * ch:4 * (ch + 1), :, :, :].opt()]), key="gcc")

    m_sct = {}
    m_scb = {}

    def emit_y_group(ch, gi, wbuf, m_wgate):
        if True:
            grp = 2 * ch + gi
            ug = grp % 2
            if gi == 0:
                eq.wait(G, m_rs[ch])
            if grp >= 2:
                eq.wait(G, m_scb[grp - 2])
            m_scr = eq.op(G, lambda g, grp=grp, ug=ug: g.dma_start(
                out=sc_stage[:, ug, :], in_=psred_rows[grp * 128:(grp + 1) * 128, :]),
                inc=16, key="gr%d" % ug)
            eq.wait(S, m_scr)
            if grp >= 2:
                eq.wait(S, m_sct.get(grp - 2) or m_sct_prev[grp - 2])
            m_scb[grp] = eq.op(S, lambda s, ug=ug: s.copy(scb_stage[:, ug, :], sc_stage[:, ug, :]))
            eq.wait(Y, m_scb[grp])
            if grp >= 2:
                eq.wait(Y, m_ymm.get(2 * (grp - 2) + 1))
            m_sct[grp] = eq.op(Y, lambda y, ug=ug: y.dma_start(
                out=sct_stage[:, ug, :].rearrange("p (t c) -> p t c", t=8),
                in_=scb_stage[:, ug, :].rearrange("p (t c) -> p t c", t=8),
                transpose=True), inc=16, key="yt")
            for k in (0, 1):
                n = 2 * grp + k
                nk = 2 * gi + k
                wv, wkey, widx = wbuf[n % len(wbuf)]
                WE = (Y if n % 2 == 0 else G) if USE_WEXPB else G
                eq.wait(WE, m_wgate)
                eq.wait(WE, eq.mark("gwc"))
                if ch == 0 and n >= 1:
                    eq.wait(WE, m_ymm[n - 1])
                if ch == 1 and n - 4 >= len(wbuf):
                    eq.wait(WE, m_ymm[n - len(wbuf)])
                if USE_WEXPB:
                    m_wx = eq.op(WE, lambda g, n=n, wv=wv: g.dma_start(
                        out=wv, in_=wexpb_d[n].rearrange("(c p) o -> p c o", p=128)),
                        inc=16, key=wkey + ("y" if n % 2 == 0 else "g"))
                else:
                    m_wx = eq.op(G, lambda g, n=n, wv=wv: g.dma_start(
                        out=wv, in_=wexp_d[n].rearrange("(c p) o -> p c o", p=128)),
                        inc=16, key=wkey)
                bbase, bh = (RS0, RS1, RS2, RS0)[nk], (0, 0, 0, 1)[nk]
                eq.wait(T, m_bex[ch])
                eq.wait(T, m_sct[grp])
                eq.wait(T, m_wx)
                t_acquire(4)
                t_acquire(5)
                colb = k * 64
                for half, bi in ((0, 4), (1, 5)):
                    for c in range(8):
                        eq.silent(T, lambda t, wv=wv, c=c, half=half, bi=bi, ug=ug, colb=colb: t.matmul(
                            pb[bi][0:64, :], sct_stage[:, ug, c * 128 + colb:c * 128 + colb + 64],
                            wv[:, c, hs(half)], start=(c == 0), stop=False, skip_group_check=True))
                    eq.op(T, lambda t, half=half, bi=bi, bbase=bbase, bh=bh: t.matmul(
                        pb[bi][0:64, :], onesb_at(bbase, 64),
                        rowsb[bbase:bbase + 1, bh * 1024 + half * H:bh * 1024 + (half + 1) * H],
                        start=False, stop=True, skip_group_check=True))
                m_ymm[n] = eq.mark(T)
                eq.wait(V, m_ymm[n])
                un = n % 2
                if n >= 2:
                    eq.wait(V, m_ydma[n - 2])
                for half, bi in ((0, 4), (1, 5)):
                    eq.op(V, lambda v, un=un, half=half, bi=bi: v.tensor_copy(
                        y_stage[0:64, un, hs(half)], pb[bi][0:64, :]))
                m_ye = eq.mark(V)
                bank_free[4] = bank_free[5] = m_ye
                eq.wait(G, m_ye)
                m_ydma[n] = eq.op(G, lambda g, n=n, un=un: g.dma_start(
                    out=yin_d.ap()[n * 64:(n + 1) * 64, :], in_=y_stage[0:64, un, :]),
                    inc=16, key="gy%d" % un)
    def emit_agy(ch):
        eq.wait(G, eq.mark("gy0"))
        eq.wait(G, eq.mark("gy1"))
        m_agy[ch] = eq.op(G, lambda g, ch=ch: g.collective_compute(
            "AllGather", ALU.bypass, replica_groups=[list(range(NCORES))],
            ins=[yin_d.ap()[256 * ch:256 * (ch + 1), :].opt()],
            outs=[yag_d.ap()[2048 * ch:2048 * (ch + 1), :].opt()]), key="gcc")

    for st in range(8):
        emit_slots_st(st)
    emit_rs(0)
    eq.wait(G, m_ffni_T)
    eq.wait(G, m_ffni_V)
    eq.wait(G, eq.mark("yt"))
    eq.wait(G, eq.mark("st"))
    wbuf0 = [(wexp0, "gx0", None)]
    m_bex = [None, None]

    def emit_bexp_rows(ch, gate):
        eq.wait(G, gate)
        mm = None
        for nk in range(4):
            bbase, bh = (RS0, RS1, RS2, RS0)[nk], (0, 0, 0, 1)[nk]
            mm = eq.op(G, lambda g, ch=ch, nk=nk, bbase=bbase, bh=bh: g.dma_start(
                out=rowsb[bbase:bbase + 1, bh * 1024:(bh + 1) * 1024],
                in_=brows_d[B_BEXP + 4 * ch + nk:B_BEXP + 4 * ch + nk + 1, :]), inc=16, key="grow")
        m_bex[ch] = mm

    emit_bexp_rows(0, m_ffni_V)
    for st in range(8, 16):
        emit_slots_st(st)
        if st == 11:
            emit_y_group(0, 0, wbuf0, m_ffni_V)
        if st == 15:
            emit_y_group(0, 1, wbuf0, m_ffni_V)
    emit_rs(1)
    emit_agy(0)
    m_slots_T = eq.mark(T)
    eq.wait(G, m_slots_T)
    wbuf1 = [(wexp[:, 0, :, :], "gx0", None), (wexp[:, 1, :, :], "gx1", None),
             (_bv(ov, OFF_H, 16384, BF16).rearrange("p (c o) -> p c o", c=8), "gx2", None),
             (_bv(ov, OFF_H + 16384, 16384, BF16).rearrange("p (c o) -> p c o", c=8), "gx3", None)]
    eq.wait(G, eq.mark(T))
    emit_bexp_rows(1, m_slots_T)
    emit_y_group(1, 0, wbuf1, m_slots_T)
    emit_y_group(1, 1, wbuf1, m_slots_T)
    emit_agy(1)

    m_yld = {}
    for ch in range(2):
        yag_r = yag_d.ap()[2048 * ch:2048 * (ch + 1), :].rearrange("(n b p) d -> n b p d", n=32, b=2)
        eq.wait(G, m_agy[ch])
        if ch == 1:
            eq.wait(Y, m_agy[ch])
        for st in range(8 * ch, 8 * ch + 8):
            for b in range(2):
                e_, k_ = ((G, "gyy") if (ch == 0 or (2 * st + b) % 2 == 0) else (Y, "gyyY"))
                m_yld[(st, b)] = eq.op(e_, lambda g, st=st, b=b, ch=ch, yag_r=yag_r: g.dma_start(
                    out=yy[:, st, b, :], in_=yag_r[4 * (st - 8 * ch):4 * (st - 8 * ch + 1), b, :, :]),
                    inc=16, key=k_)
    m_outdma = None
    m_fin_prev = None
    m_hld_prev = None
    for b in range(2):
        for mj in range(NT):
            eq.wait(S, m_fin_prev)
            m_hld = eq.op(S, lambda s, b=b, mj=mj: s.dma_start(
                out=h32_stage, in_=h32_d[b * 1024 + mj * 128:b * 1024 + (mj + 1) * 128, :]), inc=16, key="sdh")
            oa = 2 * ((b * NT + mj) % 2)
            t_acquire(oa)
            t_acquire(oa + 1)
            eq.wait(T, eq.mark("gyy"))
            eq.wait(T, eq.mark("gyyY"))
            eq.wait(T, eq.mark("yt"))
            eq.wait(T, eq.mark("st"))
            for st in range(16):
                emitfn = eq.op if st == 15 else eq.silent
                for half, bi in ((0, oa), (1, oa + 1)):
                    emitfn(T, lambda t, st=st, b=b, mj=mj, half=half, bi=bi: t.matmul(
                        pb[bi][:, :], combT[:, st, b, mj, :], yy[:, st, b, hs(half)],
                        start=(st == 0), stop=(st == 15), skip_group_check=True))
            m_mm = eq.mark(T)
            eq.wait(V, m_mm)
            eq.wait(V, m_hld)
            for half, bi in ((0, oa), (1, oa + 1)):
                eq.op(V, lambda v, half=half, bi=bi: v.tensor_add(
                    sq[:, hs(half)], pb[bi][:, :], h32_stage[:, hs(half)]))
            m_r2 = eq.mark(V)
            bank_free[oa] = bank_free[oa + 1] = m_r2
            eq.wait(S, m_r2)
            m_r2s = eq.op(S, lambda s: s.activation(h32_stage, sq, AF.Square,
                                                    accum_out=stats[:, C_VARS:C_VARS + 1]))
            eq.op(V, lambda v: v.reduce_sum(stats[:, C_SUM:C_SUM + 1], sq, axis=AX.X))
            eq.wait(V, eq.mark("V"))
            eq.op(V, lambda v: v.tensor_scalar_mul(stats[:, C_MEAN:C_MEAN + 1], stats[:, C_SUM:C_SUM + 1], 1.0 / 1024))
            eq.wait(V, eq.mark("V"))
            eq.wait(V, m_r2s)
            eq.op(V, lambda v: v.tensor_scalar(
                stats[:, C_LNV:C_LNV + 1], stats[:, C_MEAN:C_MEAN + 1], stats[:, C_MEAN:C_MEAN + 1], None,
                ALU.mult))
            eq.wait(V, eq.mark("V"))
            eq.op(V, lambda v: v.tensor_scalar(
                stats[:, C_VEPS:C_VEPS + 1], stats[:, C_VARS:C_VARS + 1], 1.0 / 1024, EPS_LN, ALU.mult, ALU.add))
            eq.wait(V, eq.mark("V"))
            m_v = eq.op(V, lambda v: v.tensor_sub(
                stats[:, C_VEPS:C_VEPS + 1], stats[:, C_VEPS:C_VEPS + 1], stats[:, C_LNV:C_LNV + 1]))
            eq.wait(S, m_v)
            eq.op(S, lambda s: s.activation(stats[:, C_LNV:C_LNV + 1], stats[:, C_VEPS:C_VEPS + 1], AF.Ln))
            eq.wait(S, eq.mark("S"))
            m_rstd = eq.op(S, lambda s: s.activation(stats[:, C_RSTD:C_RSTD + 1], stats[:, C_LNV:C_LNV + 1],
                                                     AF.Exp, scale=-0.5))
            eq.wait(V, m_rstd)
            eq.wait(V, m_outdma)
            pbuf = (b * NT + mj) % 2
            m_fin = eq.op(V, lambda v, pbuf=pbuf: v.tensor_scalar(
                x_stage[:, pbuf, :], sq, stats[:, C_MEAN:C_MEAN + 1],
                stats[:, C_RSTD:C_RSTD + 1], ALU.subtract, ALU.mult))
            m_fin_prev = m_fin
            eq.wait(S, m_fin)
            m_outdma = eq.op(S, lambda s, b=b, mj=mj, pbuf=pbuf: s.dma_start(
                out=out_d[b, mj * 128:(mj + 1) * 128, :], in_=x_stage[:, pbuf, :]), inc=16, key="sdo")

    _finalize(nc, eq, ctx, dbg, debug, h32_d=h32_d, rows=rows, d0=d0, zq=zq, kk=kk)
    return nc


def _finalize(nc, eq, ctx, dbg, debug, h32_d=None, rows=None, d0=None, zq=None, kk=None):
    G = "G"
    if debug:
        for k in list(eq.cnt):
            eq.wait(G, eq.mark(k))
        if "dbg_h" in dbg and h32_d is not None:
            eq.op(G, lambda g: g.dma_start(out=dbg["dbg_h"], in_=h32_d), inc=16, key="gdbg")
        if "dbg_rows" in dbg and rows is not None:
            eq.op(G, lambda g: g.dma_start(out=dbg["dbg_rows"][0:16, :], in_=rows[0:16, :]), inc=16, key="gdbg")
        if "dbg_d0" in dbg and d0 is not None:
            eq.op(G, lambda g: g.dma_start(
                out=dbg["dbg_d0"].rearrange("(j p) s -> p j s", p=128), in_=d0), inc=16, key="gdbg")
        if "dbg_q" in dbg and zq is not None:
            eq.op(G, lambda g: g.dma_start(
                out=dbg["dbg_q"][:, 0:1024].rearrange("(j p) s -> p j s", p=128), in_=zq), inc=16, key="gdbg")
        if "dbg_oar" in dbg and kk is not None:
            eq.op(G, lambda g: g.dma_start(
                out=dbg["dbg_oar"][:, 0:1024].rearrange("(j p) s -> p j s", p=128), in_=kk), inc=16, key="gdbg")
    finals = {k: eq.mark(k) for k in list(eq.cnt)}
    for e in eq.ENGS:
        for k, mk in finals.items():
            if k != e:
                eq.wait(e, mk)

    block = ctx.enter_context(nc.Block())

    @block.tensor
    def _(t):
        for fn in eq.q["T"]:
            fn(t)

    @block.vector
    def _(v):
        for fn in eq.q["V"]:
            fn(v)

    @block.scalar
    def _(s):
        for fn in eq.q["S"]:
            fn(s)

    @block.gpsimd
    def _(g):
        for fn in eq.q["G"]:
            fn(g)

    @block.sync
    def _(y):
        for fn in eq.q["Y"]:
            fn(y)

    ctx.close()


def _host_inputs(inputs):
    x = np.ascontiguousarray(np.asarray(inputs["L0_attn_x"], dtype=np.float32))
    Wq1 = np.asarray(inputs["L0_attn_Wq1"], np.float32)
    Wk1 = np.asarray(inputs["L0_attn_Wk1"], np.float32)
    Wk2 = np.asarray(inputs["L0_attn_Wk2"], np.float32)
    Wproj = np.asarray(inputs["L0_attn_Wproj"], np.float32)
    bq1 = np.asarray(inputs["L0_attn_bq1"], np.float32)
    bk1 = np.asarray(inputs["L0_attn_bk1"], np.float32)
    bk2 = np.asarray(inputs["L0_attn_bk2"], np.float32)
    bproj = np.asarray(inputs["L0_attn_bproj"], np.float32)
    phi = np.ascontiguousarray(np.asarray(inputs["L0_ffn_phi"], np.float32).reshape(1024, 2048))
    W = np.asarray(inputs["L0_ffn_W"], np.float32)
    bexp = np.asarray(inputs["L0_ffn_b"], np.float32)

    wcat = np.ascontiguousarray(np.concatenate([Wq1.T, Wk1.T, Wk2.T], axis=1))
    wproj = np.ascontiguousarray(Wproj.T)
    tri = np.ascontiguousarray(np.concatenate(
        [np.triu(np.ones((128, 128), np.float32)), np.ones((128, 128), np.float32)], axis=1))

    in_maps = []
    for c in range(NCORES):
        lo = c * 1024
        halo = np.zeros((2, 1, 1024), np.float32) if c == 0 else x[:, lo - 1:lo, :]
        x_pad = np.ascontiguousarray(np.concatenate([halo, x[:, lo:lo + 1024, :]], axis=1))
        brows = np.zeros((12, 1024), np.float32)
        brows[B_BQ1] = bq1
        brows[B_BK1] = bk1
        brows[B_BK2] = bk2
        brows[B_BPROJ] = bproj
        eorder = list(range(4 * c, 4 * c + 4)) + list(range(32 + 4 * c, 32 + 4 * c + 4))
        brows[B_BEXP:B_BEXP + 8] = bexp[eorder]
        pmask = np.zeros((96, 132), np.float32)
        for r in range(8):
            if r < c:
                pmask[2 * r, 0] = 1.0
                pmask[2 * r + 1, 1] = 1.0
                pmask[r, 2] = 1.0
        pmask[0, 3] = 0.0 if c == 0 else 1.0
        pmask[:, 4:132] = 1.0
        in_maps.append({
            "x": x_pad, "wcat": wcat, "wproj": wproj, "phi": phi,
            "wexp": np.ascontiguousarray(W[eorder]),
            "brows": brows, "tri": tri, "pmask": pmask,
        })
    return in_maps


_NC_CACHE = {}


def kernel(**inputs):
    in_maps = _host_inputs(inputs)
    if "full" not in _NC_CACHE:
        _NC_CACHE["full"] = build_nc()
    nc = _NC_CACHE["full"]
    res = run_bass_kernel_spmd(nc, in_maps, core_ids=list(range(NCORES)))
    out = np.empty((2, 8192, 1024), np.float32)
    for c in range(NCORES):
        out[:, c * 1024:(c + 1) * 1024, :] = res.results[c]["out"]
    return out


# version 24
# speedup vs baseline: 1.0072x; 1.0072x over previous
import contextlib
import numpy as np

import concourse.bass as bass
import concourse.mybir as mybir
from concourse.bass_utils import run_bass_kernel_spmd

F32 = mybir.dt.float32
BF16 = mybir.dt.bfloat16
U8 = mybir.dt.uint8
AF = mybir.ActivationFunctionType
ALU = mybir.AluOpType
AX = mybir.AxisListType

NCORES = 8
NT = 8
H = 512
SIGK = 6.25e-4
EPS_LN = 1e-5

RS0, RS1, RS2 = 0, 32, 64
B_BQ1, B_BK1, B_BK2, B_BPROJ = 0, 1, 2, 3
B_BEXP = 4

C_SUM, C_MEAN, C_VARS, C_VEPS, C_LNV, C_RSTD = 0, 1, 2, 3, 4, 5
C_ESUM4, C_ESUM, C_ERCP = 6, 14, 15

OV_BYTES = 180224
USE_WEXPB = True
USE_WARMUP = True
USE_EARLY_PHI = False
USE_EARLY_HT = False
OFF_WCAT = 0
OFF_WPROJ = 49152
OFF_ZQ = 65536
OFF_ZP2 = 81920
OFF_K2 = 98304
OFF_K = 114688
OFF_KV = 131072
OFF_H = 147456
OFF_PHI = 0
OFF_FSTG = 32768
OFF_COMBT = 49152
OFF_D0 = 114688
OFF_WEXP = 114688
OFF_Y = 114688


class EQ:
    ENGS = ("T", "V", "S", "G", "Y")

    def __init__(self, nc, ctx):
        self.nc = nc
        self.ctx = ctx
        self.q = {e: [] for e in self.ENGS}
        self.sems = {}
        self.cnt = {}
        self.last_wait = {e: {} for e in self.ENGS}
        for e in self.ENGS:
            self._mk(e)

    def _mk(self, name):
        if name not in self.sems:
            self.sems[name] = self.ctx.enter_context(self.nc.semaphore("sem_" + name))
            self.cnt[name] = 0

    def op(self, e, fn, inc=1, key=None):
        key = key or e
        self._mk(key)
        sem = self.sems[key]
        self.cnt[key] += inc
        val = self.cnt[key]
        selfw = (e in ("V", "S") and inc == 1 and key == e)
        def run(eng, fn=fn, sem=sem, inc=inc, val=val, selfw=selfw):
            fn(eng).then_inc(sem, inc)
            if selfw:
                eng.wait_ge(sem, val)
        self.q[e].append(run)
        return (key, val)

    def silent(self, e, fn):
        self.q[e].append(fn)

    def wait(self, e, mark):
        if mark is None:
            return
        key, val = mark
        if val is None or val <= 0:
            return
        if self.last_wait[e].get(key, 0) >= val:
            return
        self.last_wait[e][key] = val
        sem = self.sems[key]
        self.q[e].append(lambda eng, sem=sem, val=val: eng.wait_ge(sem, val))

    def mark(self, key):
        self._mk(key)
        return (key, self.cnt[key])


def _bv(ov, off, nbytes, dtype):
    return ov.ap()[:, off:off + nbytes].bitcast(dtype)


def build_nc(upto="full", debug=False):
    nc = bass.Bass("TRN2", target_bir_lowering=False)

    x_d = nc.dram_tensor("x", [2, 1025, 1024], F32, kind="ExternalInput").ap()
    wcat_d = nc.dram_tensor("wcat", [1024, 3072], F32, kind="ExternalInput").ap()
    wproj_d = nc.dram_tensor("wproj", [1024, 1024], F32, kind="ExternalInput").ap()
    phi_d = nc.dram_tensor("phi", [1024, 2048], F32, kind="ExternalInput").ap()
    wexp_d = nc.dram_tensor("wexp", [8, 1024, 1024], F32, kind="ExternalInput").ap()
    brows_d = nc.dram_tensor("brows", [12, 1024], F32, kind="ExternalInput").ap()
    tri_d = nc.dram_tensor("tri", [128, 256], F32, kind="ExternalInput").ap()
    pmask_d = nc.dram_tensor("pmask", [96, 132], F32, kind="ExternalInput").ap()
    out_d = nc.dram_tensor("out", [2, 1024, 1024], F32, kind="ExternalOutput").ap()

    h32_d = nc.dram_tensor("h32", [2048, 1024], F32, kind="Internal").ap()
    zp2_d = nc.dram_tensor("zp2b", [1025, 1024], BF16, kind="Internal").ap()
    q_d = nc.dram_tensor("qb", [1025, 1024], BF16, kind="Internal").ap()
    oar_d = nc.dram_tensor("oarb", [1025, 1024], BF16, kind="Internal").ap()
    ps_d = nc.dram_tensor("ps", [64, 2, 32, 1024], BF16, kind="Internal")
    psred_d = nc.dram_tensor("psred", [8, 2, 32, 1024], BF16, kind="Internal")
    yin_d = nc.dram_tensor("yin", [512, 1024], BF16, kind="Internal")
    yag_d = nc.dram_tensor("yag", [4096, 1024], BF16, kind="Internal")
    wexpb_d = nc.dram_tensor("wexpb", [8, 1024, 1024], BF16, kind="Internal").ap()
    wu_in = nc.dram_tensor("wu_in", [1, 128], F32, kind="Internal")
    wu_out = nc.dram_tensor("wu_out", [8, 128], F32, kind="Internal")
    agt_in = [nc.dram_tensor("agt_in%d" % b, [2, 1024], F32, kind="Internal") for b in range(2)]
    agt_out = [nc.dram_tensor("agt_out%d" % b, [16, 1024], F32, kind="Internal") for b in range(2)]
    ags_in = [nc.dram_tensor("ags_in%d" % b, [1, 1024], F32, kind="Internal") for b in range(2)]
    ags_out = [nc.dram_tensor("ags_out%d" % b, [8, 1024], F32, kind="Internal") for b in range(2)]

    dbg = {}
    if debug:
        for name, shape in [("dbg_h", [2048, 1024]), ("dbg_rows", [96, 1024]),
                            ("dbg_q", [1024, 2048]), ("dbg_oar", [1024, 2048]),
                            ("dbg_d0", [1024, 2048]), ("dbg_slots", [128, 2048])]:
            dbg[name] = nc.dram_tensor(name, shape, F32, kind="ExternalOutput").ap()

    ctx = contextlib.ExitStack()
    sb = lambda n, s, d: ctx.enter_context(nc.sbuf_tensor(n, s, d))

    ov = ctx.enter_context(nc.sbuf_tensor("ov", [128, OV_BYTES], U8))
    rows = sb("rows_s", [96, 1024], F32).ap()
    pmask2 = sb("pmask2_s", [96, 132], F32).ap()
    pmask = pmask2[0:16, 0:4]
    onesr = pmask2[:, 4:132]
    onesrb = sb("onesrb_s", [96, 128], BF16).ap()
    stats = sb("stats", [128, 18], F32).ap()
    tri2 = sb("tri_s", [128, 256], BF16).ap()
    tri_sb = tri2[:, 0:128]
    allon = tri2[:, 128:256]
    onec_sb = allon[:, 0:1]
    x_stage = sb("x_stage", [128, 2048], F32).ap().rearrange("p (u d) -> p u d", u=2)
    xb2 = sb("xb_stage", [128, 2048], BF16).ap().rearrange("p (u d) -> p u d", u=2)
    xt_stage2 = sb("xt_stage", [128, 2, 1024], BF16).ap()
    xt_stage = xt_stage2[:, 0, :]
    sq = sb("sq", [128, 1024], F32).ap()
    h32_stage = sb("h32_stage", [128, 1024], F32).ap()
    haloT = sb("haloT", [128, 256], BF16).ap().rearrange("p (b c r) -> p b c r", b=2, c=8)

    wcat = _bv(ov, OFF_WCAT, 49152, BF16).rearrange("p (c o) -> p c o", c=8)
    wproj = _bv(ov, OFF_WPROJ, 16384, BF16).rearrange("p (c o) -> p c o", c=8)
    zq = _bv(ov, OFF_ZQ, 16384, BF16).rearrange("p (j d) -> p j d", j=8)
    zp2 = _bv(ov, OFF_ZP2, 16384, BF16).rearrange("p (j d) -> p j d", j=8)
    k2 = _bv(ov, OFF_K2, 16384, BF16).rearrange("p (j d) -> p j d", j=8)
    kk = _bv(ov, OFF_K, 16384, BF16).rearrange("p (j d) -> p j d", j=8)
    kv = _bv(ov, OFF_KV, 16384, BF16).rearrange("p (j d) -> p j d", j=8)
    hh = _bv(ov, OFF_H, 32768, BF16).rearrange("p (t d) -> p t d", t=16)
    phi = _bv(ov, OFF_PHI, 32768, BF16).rearrange("p (c s) -> p c s", c=8)
    combT = _bv(ov, OFF_COMBT, 65536, BF16).rearrange("p (st b j m) -> p st b j m", st=16, b=2, j=8)
    d0 = _bv(ov, OFF_D0, 32768, BF16).rearrange("p (j s) -> p j s", j=8)
    wexp = _bv(ov, OFF_WEXP, 32768, BF16).rearrange("p (u c o) -> p u c o", u=2, c=8)
    yy = _bv(ov, OFF_Y, 65536, BF16).rearrange("p (st b e) -> p st b e", st=16, b=2)
    ht_stage = _bv(ov, OFF_FSTG, 4096, BF16).rearrange("p (u d) -> p u d", u=2)
    l0_sb = _bv(ov, OFF_FSTG + 4096, 8192, F32)
    e2_sb = _bv(ov, OFF_FSTG + 12288, 4096, BF16)
    e2x = _bv(ov, OFF_FSTG + 4096, 8192, BF16).rearrange("p (u d) -> p u d", u=2)
    rcpf = _bv(ov, OFF_FSTG + 12288, 4096, F32)
    rcpb = sq.bitcast(BF16)
    sl_stage = _bv(ov, OFF_PHI, 4096, BF16).rearrange("p (u d) -> p u d", u=2)
    d1_stage = _bv(ov, OFF_PHI + 8192, 2048, BF16)
    sc_stage = _bv(ov, OFF_PHI + 10240, 8192, F32).rearrange("p (u d) -> p u d", u=2)
    scb_stage = _bv(ov, OFF_PHI + 18432, 4096, BF16).rearrange("p (u d) -> p u d", u=2)
    sct_stage = _bv(ov, OFF_PHI + 22528, 4096, BF16).rearrange("p (u d) -> p u d", u=2)
    y_stage = _bv(ov, OFF_PHI + 26624, 4096, BF16).rearrange("p (u d) -> p u d", u=2)

    pb = [ctx.enter_context(nc.psum_tensor("pb%d" % i, [128, H], F32)) for i in range(8)]

    eq = EQ(nc, ctx)
    T, V, S, G, Y = "T", "V", "S", "G", "Y"
    def ones_at(base, n=128):
        return onesr[base:base + 1, 0:n]

    def onesb_at(base, n=128):
        return onesrb[base:base + 1, 0:n]

    rowsb = rows.bitcast(BF16)

    def biasrow(base, half):
        return rowsb[base:base + 1, half * H:(half + 1) * H]

    sqb = sq.bitcast(BF16)
    rowz_qh = sqb[0:1, 1024:2048]
    rowz_oarh = sqb[0:1, 0:1024]
    rowz_sh = rowz_oarh
    agst = rows[0:16, :]
    bank_free = [None] * 8

    def t_acquire(i):
        eq.wait(T, bank_free[i])

    def hs(half):
        return slice(half * H, (half + 1) * H)

    xbuf_free = [None, None]
    xb_cons = [[], []]

    def xb_wait(buf, eng=S):
        for mk in xb_cons[buf]:
            eq.wait(eng, mk)
        del xb_cons[buf][:]

    def load_x_tile(b, j, buf):
        eq.wait(Y, xbuf_free[buf])
        return eq.op(Y, lambda y, b=b, j=j, buf=buf: y.dma_start(
            out=x_stage[:, buf, :], in_=x_d[b, 1 + j * 128:1 + (j + 1) * 128, :]),
            inc=16, key="yx%d" % buf)

    if USE_WARMUP:
        m_wu = eq.op(G, lambda g: g.dma_start(out=wu_in.ap(), in_=pmask_d[0:1, 0:128]), inc=16, key="gwc")
        eq.wait(G, m_wu)
        eq.op(G, lambda g: g.collective_compute(
            "AllGather", ALU.bypass, replica_groups=[list(range(NCORES))],
            ins=[wu_in.ap().opt()], outs=[wu_out.ap().opt()]), key="gcc")
    eq.op(G, lambda g: g.dma_start(out=onesrb, in_=pmask_d[:, 4:132]), inc=16, key="gw")
    m_tri = eq.op(G, lambda g: g.dma_start(out=tri2, in_=tri_d), inc=16, key="gw")
    m_onec = m_tri
    m_pmask = eq.op(G, lambda g: g.dma_start(out=pmask2, in_=pmask_d), inc=16, key="gw")
    m_wcat = eq.op(G, lambda g: g.dma_start(
        out=wcat, in_=wcat_d.rearrange("(c p) o -> p c o", p=128)), inc=16, key="gw")
    m_wproj = eq.op(G, lambda g: g.dma_start(
        out=wproj, in_=wproj_d.rearrange("(c p) o -> p c o", p=128)), inc=16, key="gw")
    m_gw0 = eq.mark("gw")
    m_rows = m_onesr = m_tri = m_onec = m_pmask = m_wcat = m_wproj = m_gw0

    m_ms = eq.op(V, lambda v: v.memzero(xb2[:, 0, :]))
    eq.wait(G, m_ms)
    m_hl = [eq.op(G, lambda g, b=b: g.dma_start(
        out=xb2[b * 32:b * 32 + 1, 0, :], in_=x_d[b, 0:1, :]), inc=16, key="ghl") for b in range(2)]
    eq.wait(Y, m_hl[1])
    m_htp = [eq.op(Y, lambda y, b=b: y.dma_start(
        out=haloT[:, b, :, :],
        in_=xb2[b * 32:b * 32 + 16, 0, :].rearrange("p (c f) -> p c f", c=8),
        transpose=True), inc=16, key="yt") for b in range(2)]
    xb_cons[0].append(m_htp[1])

    def emit_bias_slots(b):
        eq.wait(G, eq.mark(T))
        m = []
        for bsrc, rdst in ((B_BQ1, RS0), (B_BK1, RS1), (B_BK2, RS2)):
            m.append(eq.op(G, lambda g, bsrc=bsrc, rdst=rdst: g.dma_start(
                out=rowsb[rdst:rdst + 1, 0:1024], in_=brows_d[bsrc:bsrc + 1, :]), inc=16, key="grow"))
        return m[-1]

    def emit_halo_proj(b, m_bias):
        eq.wait(T, m_htp[1])
        eq.wait(T, m_wcat)
        eq.wait(T, m_bias)
        for pi, (o_base, r_b) in ((0, (0, RS0)), (1, (2048, RS2))):
            for half in range(2):
                bi = 4 + 2 * pi + half
                t_acquire(bi)
                for c in range(8):
                    eq.silent(T, lambda t, b=b, c=c, o=o_base, half=half, bi=bi: t.matmul(
                        pb[bi][0:1, :], haloT[:, b, c, 0:1],
                        wcat[:, c, o + half * H:o + half * H + H],
                        start=(c == 0), stop=False, skip_group_check=True))
                eq.op(T, lambda t, r_b=r_b, half=half, bi=bi: t.matmul(
                    pb[bi][0:1, :], onesb_at(r_b, 1), biasrow(r_b, half),
                    start=False, stop=True, skip_group_check=True))
        m_mm = eq.mark(T)
        eq.wait(V, m_mm)
        for half in range(2):
            eq.op(V, lambda v, half=half: v.tensor_copy(rowz_qh[:, hs(half)], pb[4 + half][0:1, :]))
            eq.op(V, lambda v, half=half: v.tensor_copy(k2[0:1, 7, hs(half)], pb[6 + half][0:1, :]))
        m_e = eq.mark(V)
        for i in range(4, 8):
            bank_free[i] = m_e
        return m_e

    def emit_pass1a(b, m_hprev):
        for i in range(4, 8):
            t_acquire(i)
        eq.wait(V, m_hprev)
        xm = [load_x_tile(b, 0, 0)]
        m_cast = [None] * NT
        m_xt = [None] * NT
        m_kvv = [None] * NT
        m_projlast = None
        m_proj2last = None
        for j in range(NT):
            buf = j % 2
            eq.wait(V, xm[j])
            xb_wait(buf, V)
            m_cast[j] = eq.op(V, lambda v, buf=buf: v.tensor_copy(xb2[:, buf, :], x_stage[:, buf, :]))
            if j + 1 < NT:
                xbuf_free[buf] = m_cast[j]
                xm.append(load_x_tile(b, j + 1, 1 - buf))
            eq.wait(Y, m_cast[j])
            if j >= 2:
                eq.wait(Y, m_proj2last)
            xu = j % 2
            m_xt[j] = eq.op(Y, lambda y, xu=xu, buf=buf: y.dma_start(
                out=xt_stage2[:, xu, :].rearrange("p (t c) -> p t c", t=8),
                in_=xb2[:, buf, :].rearrange("p (t c) -> p t c", t=8),
                transpose=True), inc=16, key="yt%d" % xu)
            xb_cons[buf].append(m_xt[j])
            eq.wait(T, m_xt[j])
            for pi in range(3):
                for half in range(2):
                    bi = (2 * pi + half) % 4
                    t_acquire(bi)
                    o0 = pi * 1024 + half * H
                    for c in range(8):
                        eq.silent(T, lambda t, c=c, o0=o0, bi=bi, xu=xu: t.matmul(
                            pb[bi][:, :], xt_stage2[:, xu, c * 128:(c + 1) * 128],
                            wcat[:, c, o0:o0 + H], start=(c == 0), stop=False,
                            skip_group_check=True))
                    r_b = (RS0, RS1, RS2)[pi]
                    m_mm = eq.op(T, lambda t, r_b=r_b, half=half, bi=bi: t.matmul(
                        pb[bi][:, :], onesb_at(r_b), biasrow(r_b, half),
                        start=False, stop=True, skip_group_check=True))
                    eq.wait(S, m_mm)
                    if pi == 0:
                        fn = lambda s, j=j, half=half, bi=bi: s.copy(zq[:, j, hs(half)], pb[bi][:, :])
                    elif pi == 1:
                        fn = lambda s, j=j, half=half, bi=bi: s.activation(
                            kk[:, j, hs(half)], pb[bi][:, :], AF.Exp)
                    else:
                        fn = lambda s, j=j, half=half, bi=bi: s.copy(zp2[:, j, hs(half)], pb[bi][:, :])
                    m_ev = eq.op(S, fn)
                    bank_free[bi] = m_ev
                    if pi == 2 and half == 1:
                        eq.wait(G, m_ev)
                        eq.op(G, lambda g, j=j: g.dma_start(
                            out=zp2_d[1 + j * 128:1 + (j + 1) * 128, :], in_=zp2[:, j, :]),
                            inc=16, key="gsh")
                if pi == 1:
                    eq.wait(V, eq.mark(S))
                    m_kvv[j] = eq.op(V, lambda v, j=j, buf=buf: v.tensor_mul(kv[:, j, :], kk[:, j, :], xb2[:, buf, :]))
                    xb_cons[buf].append(m_kvv[j])
            m_proj2last = m_projlast
            m_projlast = eq.mark(T)
        eq.wait(T, m_onec)
        for j in range(NT):
            eq.wait(T, m_kvv[j])
            for half in range(2):
                eq.op(T, lambda t, j=j, half=half: t.matmul(
                    pb[4 + half][0:1, :], onec_sb, kk[:, j, hs(half)],
                    start=(j == 0), stop=(j == NT - 1), skip_group_check=True))
                eq.op(T, lambda t, j=j, half=half: t.matmul(
                    pb[6 + half][0:1, :], onec_sb, kv[:, j, hs(half)],
                    start=(j == 0), stop=(j == NT - 1), skip_group_check=True))
        m_tot = eq.mark(T)
        eq.wait(V, m_tot)
        for half in range(2):
            eq.op(V, lambda v, half=half: v.tensor_copy(rows[0:1, hs(half)], pb[4 + half][0:1, :]))
            eq.op(V, lambda v, half=half: v.tensor_copy(h32_stage[0:1, hs(half)], pb[6 + half][0:1, :]))
        m_ev = eq.mark(V)
        for i in range(4, 8):
            bank_free[i] = m_ev
        eq.wait(G, m_ev)
        eq.op(G, lambda g, b=b: g.dma_start(out=agt_in[b].ap()[0:1, :], in_=rows[0:1, :]), inc=16, key="gag")
        eq.op(G, lambda g, b=b: g.dma_start(out=agt_in[b].ap()[1:2, :], in_=h32_stage[0:1, :]), inc=16, key="gag")
        eq.wait(G, eq.mark("gag"))
        eq.op(G, lambda g, b=b: g.collective_compute(
            "AllGather", ALU.bypass, replica_groups=[list(range(NCORES))],
            ins=[agt_in[b].ap().opt()], outs=[agt_out[b].ap().opt()]), key="gcc")
        eq.wait(G, eq.mark("gcc"))
        m_rd = eq.op(G, lambda g, b=b: g.dma_start(out=agst, in_=agt_out[b].ap()), inc=16, key="gag")
        return m_rd

    def emit_shift_zp2(b):
        eq.wait(G, eq.mark(V))
        eq.op(G, lambda g: g.dma_start(out=zp2_d[0:1, :], in_=k2[0:1, 7, :]), inc=16, key="gsh")
        eq.wait(G, eq.mark("gsh"))
        m = eq.op(G, lambda g: g.dma_start(
            out=k2, in_=zp2_d[0:1024, :].rearrange("(j p) d -> p j d", p=128)), inc=16, key="gsh")
        return m

    def emit_sigmoid_batch(b, m_k2rd):
        eq.wait(S, eq.mark(V))
        eq.op(S, lambda s: s.activation(rowz_qh, rowz_qh, AF.Sigmoid))
        for j in range(NT):
            eq.op(S, lambda s, j=j: s.activation(zq[:, j, :], zq[:, j, :], AF.Sigmoid))
        m_sig = eq.mark(S)
        eq.wait(S, m_k2rd)
        for j in range(NT):
            eq.op(S, lambda s, j=j: s.activation(k2[:, j, :], k2[:, j, :], AF.Sigmoid, scale=SIGK))
        m_sigk2 = eq.mark(S)
        eq.wait(G, m_k2rd)
        eq.wait(G, m_sig)
        eq.op(G, lambda g: g.dma_start(
            out=q_d[1:1025, :].rearrange("(j p) d -> p j d", p=128), in_=zq), inc=16, key="gsh")
        m_qw = eq.op(G, lambda g: g.dma_start(out=q_d[0:1, :], in_=rowz_qh), inc=16, key="gsh")
        return m_sig, m_sigk2, m_qw

    def emit_carry_setup(b, m_ag, m_sig):
        eq.wait(T, m_ag)
        eq.wait(T, m_pmask)
        for i in (4, 5, 6, 7):
            t_acquire(i)
        for half in range(2):
            eq.op(T, lambda t, half=half: t.matmul(
                pb[4 + half][0:1, :], pmask[0:16, 0:1], agst[:, hs(half)],
                start=True, stop=True, skip_group_check=True))
            eq.op(T, lambda t, half=half: t.matmul(
                pb[6 + half][0:1, :], pmask[0:16, 1:2], agst[:, hs(half)],
                start=True, stop=True, skip_group_check=True))
        m_mm = eq.mark(T)
        eq.wait(V, m_mm)
        for half in range(2):
            eq.op(V, lambda v, half=half: v.tensor_scalar_add(
                rows[RS0:RS0 + 1, hs(half)], pb[4 + half][0:1, :], 1e-6))
            eq.op(V, lambda v, half=half: v.tensor_copy(h32_stage[0:1, hs(half)], pb[6 + half][0:1, :]))
        m_gkv = eq.mark(V)
        eq.wait(G, m_gkv)
        m_ckv = eq.op(G, lambda g: g.dma_start(out=rows[RS1:RS1 + 1, :], in_=h32_stage[0:1, :]), inc=16, key="grow")
        eq.wait(V, m_ckv)
        eq.op(V, lambda v: v.reciprocal(h32_stage[0:1, :], rows[RS0:RS0 + 1, :]))
        eq.wait(V, eq.mark("V"))
        for half in range(2):
            eq.op(V, lambda v, half=half: v.tensor_mul(h32_stage[0:1, hs(half)], pb[6 + half][0:1, :], h32_stage[0:1, hs(half)]))
        m_sh = eq.op(V, lambda v: v.tensor_copy(rowz_sh, h32_stage[0:1, :]))
        bank_free[4] = bank_free[5] = bank_free[6] = bank_free[7] = m_sh
        eq.wait(V, m_sig)
        m_oarh = eq.op(V, lambda v: v.tensor_mul(rowz_oarh, rowz_oarh, rowz_qh))
        return m_oarh, m_ckv

    def emit_pass1b(b, m_carry, m_ckv, m_sig, m_hprev):
        eq.wait(V, m_hprev)
        eq.wait(T, m_tri)
        eq.wait(T, m_carry)
        eq.wait(T, m_ckv)
        eq.wait(V, m_sig)
        for j in range(NT):
            for qi in range(2):
                srcv = kk if qi == 0 else kv
                carry_r = RS0 if qi == 0 else RS1
                base = 4 * (j % 2)
                b0, b1 = (base, base + 1) if qi == 0 else (base + 2, base + 3)
                t_acquire(b0)
                t_acquire(b1)
                for half, bi in ((0, b0), (1, b1)):
                    eq.silent(T, lambda t, j=j, half=half, srcv=srcv, bi=bi: t.matmul(
                        pb[bi][:, :], tri_sb, srcv[:, j, hs(half)],
                        start=True, stop=False, skip_group_check=True))
                    for i in range(j):
                        eq.silent(T, lambda t, i=i, half=half, srcv=srcv, bi=bi: t.matmul(
                            pb[bi][:, :], allon, srcv[:, i, hs(half)],
                            start=False, stop=False, skip_group_check=True))
                for half, bi in ((0, b0), (1, b1)):
                    eq.op(T, lambda t, half=half, carry_r=carry_r, bi=bi: t.matmul(
                        pb[bi][:, :], ones_at(carry_r), rows[carry_r:carry_r + 1, hs(half)],
                        start=False, stop=True, skip_group_check=True))
                m_mm = eq.mark(T)
                eq.wait(V, m_mm)
                if qi == 0:
                    for half, bi in ((0, b0), (1, b1)):
                        eq.op(V, lambda v, half=half, bi=bi: v.reciprocal(
                            h32_stage[:, hs(half)], pb[bi][:, :]))
                    mk = eq.mark(V)
                    bank_free[b0] = bank_free[b1] = mk
                else:
                    eq.wait(V, eq.mark("V"))
                    for half, bi in ((0, b0), (1, b1)):
                        eq.op(V, lambda v, half=half, bi=bi: v.tensor_mul(
                            xt_stage[:, hs(half)], pb[bi][:, :], h32_stage[:, hs(half)]))
                    mk = eq.mark(V)
                    bank_free[b0] = bank_free[b1] = mk
                    eq.op(V, lambda v, j=j: v.tensor_mul(zp2[:, j, :], xt_stage, zq[:, j, :]))
        return eq.mark(V)

    def emit_shift_oar_q(b, m_p1b):
        eq.wait(G, m_p1b)
        eq.op(G, lambda g: g.dma_start(
            out=oar_d[1:1025, :].rearrange("(j p) d -> p j d", p=128), in_=zp2), inc=16, key="gsh")
        eq.op(G, lambda g: g.dma_start(out=oar_d[0:1, :], in_=rowz_oarh), inc=16, key="gsh")
        eq.wait(G, eq.mark("gsh"))
        m_oar = eq.op(G, lambda g: g.dma_start(
            out=kk, in_=oar_d[0:1024, :].rearrange("(j p) d -> p j d", p=128)), inc=16, key="gsh")
        m_q = eq.op(G, lambda g: g.dma_start(
            out=zq, in_=q_d[0:1024, :].rearrange("(j p) d -> p j d", p=128)), inc=16, key="gsh")
        return m_q, m_q

    def emit_pass2a(b, m_oarsh, m_sigk2):
        t_acquire(6)
        t_acquire(7)
        eq.wait(V, m_oarsh)
        eq.wait(V, m_sigk2)
        xm = [load_x_tile(b, 0, 0)]
        m_k2e = [None] * NT
        m_cast = [None] * NT
        for j in range(NT):
            buf = j % 2
            eq.wait(S, xm[j])
            xb_wait(buf)
            m_cast[j] = eq.op(S, lambda s, buf=buf: s.copy(xb2[:, buf, :], x_stage[:, buf, :]))
            if j + 1 < NT:
                xbuf_free[buf] = m_cast[j]
                xm.append(load_x_tile(b, j + 1, 1 - buf))
            eq.wait(V, m_cast[j])
            eq.op(V, lambda v, j=j, buf=buf: v.tensor_sub(xb2[:, buf, :], xb2[:, buf, :], kk[:, j, :]))
            if j == 0:
                eq.op(V, lambda v: v.tensor_scalar_mul(xb2[0:1, 0, :], xb2[0:1, 0, :], pmask[0:1, 3:4]))
            m_k2e[j] = eq.op(V, lambda v, j=j, buf=buf: v.tensor_mul(kv[:, j, :], k2[:, j, :], xb2[:, buf, :]))
            xb_cons[buf].append(m_k2e[j])
            eq.wait(T, m_k2e[j])
            for half in range(2):
                eq.op(T, lambda t, j=j, half=half: t.matmul(
                    pb[6 + half][0:1, :], onec_sb, kv[:, j, hs(half)],
                    start=(j == 0), stop=(j == NT - 1), skip_group_check=True))
        xbuf_free[1] = m_cast[NT - 1]
        m_tot = eq.mark(T)
        eq.wait(V, m_tot)
        for half in range(2):
            eq.op(V, lambda v, half=half: v.tensor_copy(rows[0:1, hs(half)], pb[6 + half][0:1, :]))
        m_ev = eq.mark(V)
        bank_free[6] = bank_free[7] = m_ev
        eq.wait(G, m_ev)
        eq.op(G, lambda g, b=b: g.dma_start(out=ags_in[b].ap(), in_=rows[0:1, :]), inc=16, key="gag")
        eq.wait(G, eq.mark("gag"))
        eq.op(G, lambda g, b=b: g.collective_compute(
            "AllGather", ALU.bypass, replica_groups=[list(range(NCORES))],
            ins=[ags_in[b].ap().opt()], outs=[ags_out[b].ap().opt()]), key="gcc")
        eq.wait(G, eq.mark("gcc"))
        m_rd = eq.op(G, lambda g, b=b: g.dma_start(out=agst[0:8, :], in_=ags_out[b].ap()), inc=16, key="gag")
        return m_rd

    def emit_pass2b(b, m_ag, m_qsh):
        eq.wait(T, m_ag)
        t_acquire(4)
        t_acquire(5)
        for half in range(2):
            eq.op(T, lambda t, half=half: t.matmul(
                pb[4 + half][0:1, :], pmask[0:8, 2:3], agst[0:8, hs(half)],
                start=True, stop=True, skip_group_check=True))
        m_mm = eq.mark(T)
        eq.wait(V, m_mm)
        for half in range(2):
            eq.op(V, lambda v, half=half: v.tensor_copy(rows[RS0:RS0 + 1, hs(half)], pb[4 + half][0:1, :]))
        m_cs = eq.mark(V)
        bank_free[4] = bank_free[5] = m_cs
        eq.wait(G, eq.mark(T))
        m_bp = eq.op(G, lambda g: g.dma_start(out=rowsb[RS1:RS1 + 1, 0:1024], in_=brows_d[B_BPROJ:B_BPROJ + 1, :]), inc=16, key="grow")
        eq.wait(T, m_cs)
        eq.wait(V, m_qsh)
        m_h32dma = None
        m_at = [None] * NT
        xm = []
        for j in range(NT + 1):
            if j < NT:
                buf = j % 2
                if j == 0:
                    xm.append(load_x_tile(b, 0, 0))
                    xm.append(load_x_tile(b, 1, 1))
                s2a = 4 * (j % 2)
                t_acquire(s2a)
                t_acquire(s2a + 1)
                for half, bi in ((0, s2a), (1, s2a + 1)):
                    eq.silent(T, lambda t, j=j, half=half, bi=bi: t.matmul(
                        pb[bi][:, :], tri_sb, kv[:, j, hs(half)],
                        start=True, stop=False, skip_group_check=True))
                    for i in range(j):
                        eq.silent(T, lambda t, i=i, half=half, bi=bi: t.matmul(
                            pb[bi][:, :], allon, kv[:, i, hs(half)],
                            start=False, stop=False, skip_group_check=True))
                for half, bi in ((0, s2a), (1, s2a + 1)):
                    eq.op(T, lambda t, half=half, bi=bi: t.matmul(
                        pb[bi][:, :], ones_at(RS0), rows[RS0:RS0 + 1, hs(half)],
                        start=False, stop=True, skip_group_check=True))
                m_mm = eq.mark(T)
                eq.op(V, lambda v, j=j: v.tensor_scalar_mul(sq[1:128, :], zq[0:127, j, :], SIGK)
                      if False else v.tensor_scalar_mul(sq, zq[:, j, :], SIGK))
                if j == 0:
                    eq.op(V, lambda v: v.tensor_scalar_mul(sq[0:1, :], sq[0:1, :], pmask[0:1, 3:4]))
                eq.wait(V, m_mm)
                xu = j % 2
                for half, bi in ((0, s2a), (1, s2a + 1)):
                    eq.op(V, lambda v, half=half, bi=bi, xu=xu: v.tensor_mul(
                        xt_stage2[:, xu, hs(half)], pb[bi][:, :], sq[:, hs(half)]))
                mk = eq.mark(V)
                bank_free[s2a] = bank_free[s2a + 1] = mk
                m_a = eq.op(V, lambda v, j=j, xu=xu: v.tensor_add(kk[:, j, :], xt_stage2[:, xu, :], zp2[:, j, :]))
                eq.wait(Y, m_a)
                if j >= 2:
                    eq.wait(Y, m_pj2)
                m_at[j] = eq.op(Y, lambda y, j=j, xu=xu: y.dma_start(
                    out=xt_stage2[:, xu, :].rearrange("p (t c) -> p t c", t=8),
                    in_=kk[:, j, :].rearrange("p (t c) -> p t c", t=8),
                    transpose=True), inc=16, key="yt%d" % xu)
            if j >= 1:
                pj = j - 1
                buf = pj % 2
                xu = pj % 2
                eq.wait(T, m_at[pj])
                eq.wait(T, m_wproj)
                eq.wait(T, m_bp)
                pja = 4 * (pj % 2) + 2
                for half, bi in ((0, pja), (1, pja + 1)):
                    t_acquire(bi)
                    for c in range(8):
                        eq.silent(T, lambda t, c=c, half=half, bi=bi, xu=xu: t.matmul(
                            pb[bi][:, :], xt_stage2[:, xu, c * 128:(c + 1) * 128],
                            wproj[:, c, hs(half)], start=(c == 0), stop=False, skip_group_check=True))
                    eq.op(T, lambda t, half=half, bi=bi: t.matmul(
                        pb[bi][:, :], onesb_at(RS1), biasrow(RS1, half),
                        start=False, stop=True, skip_group_check=True))
                m_pj2 = eq.mark(T)
                eq.wait(V, m_pj2)
                eq.wait(V, xm[pj])
                eq.wait(V, m_h32dma)
                for half, bi in ((0, pja), (1, pja + 1)):
                    eq.op(V, lambda v, half=half, bi=bi, buf=buf: v.tensor_add(
                        h32_stage[:, hs(half)], pb[bi][:, :], x_stage[:, buf, hs(half)]))
                m_r = eq.mark(V)
                bank_free[pja] = bank_free[pja + 1] = m_r
                xbuf_free[buf] = m_r
                if pj + 2 < NT:
                    xm.append(load_x_tile(b, pj + 2, buf))
                eq.wait(S, m_r)
                m_rsq = eq.op(S, lambda s: s.activation(sq, h32_stage, AF.Square,
                                                        accum_out=stats[:, C_VARS:C_VARS + 1]))
                eq.op(V, lambda v: v.reduce_sum(stats[:, C_SUM:C_SUM + 1], h32_stage, axis=AX.X))
                eq.wait(V, eq.mark("V"))
                eq.op(V, lambda v: v.tensor_scalar_mul(stats[:, C_MEAN:C_MEAN + 1], stats[:, C_SUM:C_SUM + 1], 1.0 / 1024))
                eq.wait(V, eq.mark("V"))
                eq.wait(V, m_rsq)
                eq.op(V, lambda v: v.tensor_scalar(
                    stats[:, C_LNV:C_LNV + 1], stats[:, C_MEAN:C_MEAN + 1], stats[:, C_MEAN:C_MEAN + 1], None,
                    ALU.mult))
                eq.wait(V, eq.mark("V"))
                eq.op(V, lambda v: v.tensor_scalar(
                    stats[:, C_VEPS:C_VEPS + 1], stats[:, C_VARS:C_VARS + 1], 1.0 / 1024, EPS_LN, ALU.mult, ALU.add))
                eq.wait(V, eq.mark("V"))
                m_v = eq.op(V, lambda v: v.tensor_sub(
                    stats[:, C_VEPS:C_VEPS + 1], stats[:, C_VEPS:C_VEPS + 1], stats[:, C_LNV:C_LNV + 1]))
                eq.wait(S, m_v)
                eq.op(S, lambda s: s.activation(stats[:, C_LNV:C_LNV + 1], stats[:, C_VEPS:C_VEPS + 1], AF.Ln))
                eq.wait(S, eq.mark("S"))
                m_rs = eq.op(S, lambda s: s.activation(stats[:, C_RSTD:C_RSTD + 1], stats[:, C_LNV:C_LNV + 1],
                                                       AF.Exp, scale=-0.5))
                eq.wait(V, m_rs)
                eq.op(V, lambda v: v.tensor_scalar(
                    h32_stage, h32_stage, stats[:, C_MEAN:C_MEAN + 1],
                    stats[:, C_RSTD:C_RSTD + 1], ALU.subtract, ALU.mult))
                m_h32 = eq.op(V, lambda v, b=b, pj=pj: v.tensor_copy(hh[:, b * 8 + pj, :], h32_stage))
                eq.wait(S, m_h32)
                m_h32dma = eq.op(S, lambda s, b=b, pj=pj: s.dma_start(
                    out=h32_d[b * 1024 + pj * 128:b * 1024 + (pj + 1) * 128, :], in_=h32_stage), inc=16, key="sdh")
        return m_h32dma

    def emit_wexp_cast(n):
        eq.op(G, lambda g, n=n: g.dma_start(out=wexpb_d[n], in_=wexp_d[n]),
              inc=16, key="gwc")

    h_done = []
    m_phi = None
    if not USE_EARLY_PHI:
        pass
    for b in range(2):
        m_bias = emit_bias_slots(b)
        emit_halo_proj(b, m_bias)
        m_ag1 = emit_pass1a(b, h_done[-1] if h_done else None)
        if b == 1:
            m_wcat_T_free = eq.mark(T)
        m_k2rd = emit_shift_zp2(b)
        m_sig, m_sigk2, m_qw = emit_sigmoid_batch(b, m_k2rd)
        m_carry, m_ckv = emit_carry_setup(b, m_ag1, m_sig)
        m_p1b = emit_pass1b(b, m_carry, m_ckv, m_sig, h_done[-1] if h_done else None)
        m_oarsh, m_qsh = emit_shift_oar_q(b, m_p1b)
        m_ag2 = emit_pass2a(b, m_oarsh, m_sigk2)
        if USE_WEXPB:
            for n in range(4 * b, 4 * b + 4):
                emit_wexp_cast(n)
        if b == 1 and USE_EARLY_PHI:
            eq.wait(G, m_wcat_T_free)
            m_phi = eq.op(G, lambda g: g.dma_start(
                out=phi, in_=phi_d.rearrange("(c p) s -> p c s", p=128)), inc=16, key="gw")
        m_h = emit_pass2b(b, m_ag2, m_qsh)
        h_done.append(m_h)
    del b
    m_att_T = eq.mark(T)
    m_att_S = eq.mark(S)
    if not USE_EARLY_PHI:
        eq.wait(G, eq.mark(T))
        eq.wait(G, eq.mark(S))
        m_phi = eq.op(G, lambda g: g.dma_start(
            out=phi, in_=phi_d.rearrange("(c p) s -> p c s", p=128)), inc=16, key="gw")

    if upto == "attn":
        _finalize(nc, eq, ctx, dbg, debug, h32_d=h32_d, rows=rows, d0=d0, zq=zq, kk=kk)
        return nc

    m_e2_free = [None, None]
    m_l_prev = [None, None]
    m_d0_prev = None
    eq.wait(V, eq.mark("sdh"))
    for mj in range(NT):
        m_ht = []
        eq.wait(Y, eq.mark("yt"))
        for b in range(2):
            eq.wait(Y, h_done[1])
            eq.wait(Y, m_phi)
            eq.wait(Y, m_l_prev[b])
            m_ht.append(eq.op(Y, lambda y, b=b, mj=mj: y.dma_start(
                out=ht_stage[:, b, :].rearrange("p (t c) -> p t c", t=8),
                in_=hh[:, b * 8 + mj, :].rearrange("p (t c) -> p t c", t=8),
                transpose=True), inc=16, key="yt"))
        eq.wait(T, m_phi)
        m_l = [None, None]
        m_e2 = [None, None]
        for b in range(2):
            eq.wait(T, m_ht[1])
            for ch in range(4):
                bi = 4 * b + ch
                t_acquire(bi)
                for c in range(8):
                    emitfn = eq.op if c == 7 else eq.silent
                    emitfn(T, lambda t, b=b, c=c, ch=ch, bi=bi: t.matmul(
                        pb[bi][:, :], ht_stage[:, b, c * 128:(c + 1) * 128],
                        phi[:, c, ch * H:(ch + 1) * H], start=(c == 0), stop=(c == 7),
                        skip_group_check=True))
            m_l[b] = eq.mark(T)
            eq.wait(S, m_l[b])
            eq.wait(S, m_e2_free[b])
            for ch in range(4):
                eq.op(S, lambda s, b=b, ch=ch: s.activation(
                    e2x[:, b, ch * H:(ch + 1) * H], pb[4 * b + ch][:, :], AF.Exp,
                    accum_out=stats[:, C_ESUM4 + 4 * b + ch:C_ESUM4 + 4 * b + ch + 1]))
            m_e2[b] = eq.mark(S)
            for ch in range(4):
                bank_free[4 * b + ch] = m_e2[b]
        m_l_prev = m_l
        eq.wait(V, m_e2[1])
        eq.wait(V, m_d0_prev)
        m_a0 = eq.op(V, lambda v: v.tensor_add(rcpf, e2x[:, 0, 0:1024], e2x[:, 1, 0:1024]))
        m_a1 = eq.op(V, lambda v: v.tensor_add(h32_stage, e2x[:, 0, 1024:2048], e2x[:, 1, 1024:2048]))
        eq.wait(S, m_a1)
        eq.op(S, lambda s: s.activation(rcpf, rcpf, AF.Ln))
        eq.op(S, lambda s: s.activation(rcpb[:, 0:1024], rcpf, AF.Exp, scale=-1.0))
        eq.op(S, lambda s: s.activation(h32_stage, h32_stage, AF.Ln))
        eq.op(S, lambda s: s.activation(rcpb[:, 1024:2048], h32_stage, AF.Exp, scale=-1.0))
        m_dS = eq.mark(S)
        eq.wait(V, m_dS)
        m_d0_prev = eq.op(V, lambda v, mj=mj: v.tensor_mul(d0[:, mj, :], e2x[:, 0, :], rcpb))
        for b in range(2):
            eq.wait(V, m_e2[b])
            eq.op(V, lambda v, b=b: v.reduce_sum(
                stats[:, C_ESUM:C_ESUM + 1], stats[:, C_ESUM4 + 4 * b:C_ESUM4 + 4 * b + 4], axis=AX.X))
            eq.wait(V, eq.mark("V"))
            eq.op(V, lambda v: v.reciprocal(stats[:, C_ERCP:C_ERCP + 1], stats[:, C_ESUM:C_ESUM + 1]))
            eq.wait(V, eq.mark("V"))
            m_cb = eq.op(V, lambda v, b=b: v.tensor_scalar_mul(e2x[:, b, :], e2x[:, b, :], stats[:, C_ERCP:C_ERCP + 1]))
            eq.wait(S, m_cb)
            m_comb_tp = eq.op(S, lambda s, b=b, mj=mj: s.dma_start(
                out=combT[:, :, b, mj, :],
                in_=e2x[:, b, :].rearrange("p (t c) -> p t c", t=16),
                transpose=True), inc=16, key="st")
            if b == 1:
                m_e2_free = [m_comb_tp, m_comb_tp]
    m_ffni_V = eq.mark(V)
    m_ffni_Y = eq.mark(Y)
    m_ffni_T = eq.mark(T)

    eq.wait(T, m_ffni_V)
    eq.wait(S, m_ffni_V)
    eq.wait(V, m_ffni_T)
    wexp0 = _bv(ov, OFF_FSTG, 16384, BF16).rearrange("p (c o) -> p c o", c=8)
    psred_rows = psred_d.ap().rearrange("n b p d -> (n b p) d")
    sl_dma = []
    m_slots_T_b1 = None
    m_rs = [None, None]
    m_agy = [None, None]
    m_ymm = {}
    m_ydma = {}

    slots_gate = {"b1": None}

    def emit_slots_st(st):
        nonlocal_marks = slots_gate
        if True:
            for b in range(2):
                u = b
                if b == 1:
                    eq.wait(S, nonlocal_marks.get("b1"))
                    for mj in range(NT):
                        eq.op(S, lambda s, st=st, mj=mj: s.activation(
                            d1_stage[:, mj * 128:(mj + 1) * 128], d0[:, mj, st * 128:(st + 1) * 128],
                            AF.Copy, bias=1.0, scale=-1.0))
                m_d1 = eq.mark(S)
                bi0, bi1 = 2 * b, 2 * b + 1
                t_acquire(bi0)
                t_acquire(bi1)
                if b == 1:
                    eq.wait(T, m_d1)
                for mj in range(NT):
                    emitfn = eq.op if mj == NT - 1 else eq.silent
                    for half, bi in ((0, bi0), (1, bi1)):
                        if b == 0:
                            emitfn(T, lambda t, st=st, mj=mj, half=half, bi=bi: t.matmul(
                                pb[bi][:, :], d0[:, mj, st * 128:(st + 1) * 128],
                                hh[:, mj, hs(half)],
                                start=(mj == 0), stop=(mj == NT - 1), skip_group_check=True))
                        else:
                            emitfn(T, lambda t, st=st, mj=mj, half=half, bi=bi: t.matmul(
                                pb[bi][:, :], d1_stage[:, mj * 128:(mj + 1) * 128],
                                hh[:, 8 + mj, hs(half)],
                                start=(mj == 0), stop=(mj == NT - 1), skip_group_check=True))
                m_mm = eq.mark(T)
                if b == 1:
                    nonlocal_marks["b1"] = m_mm
                eq.wait(V, m_mm)
                if len(sl_dma) >= 2:
                    eq.wait(V, sl_dma[-2])
                for half, bi in ((0, bi0), (1, bi1)):
                    eq.op(V, lambda v, u=u, half=half, bi=bi: v.tensor_copy(sl_stage[:, u, hs(half)], pb[bi][:, :]))
                m_ev = eq.mark(V)
                bank_free[bi0] = bank_free[bi1] = m_ev
                eq.wait(G, m_ev)
                sl_dma.append(eq.op(G, lambda g, st=st, b=b, u=u: g.dma_start(
                    out=ps_d.ap()[4 * st:4 * (st + 1), b, :, :], in_=sl_stage[:, u, :]),
                    inc=16, key="gp%d" % u))

    def emit_rs(ch):
        eq.wait(G, eq.mark("gp0"))
        eq.wait(G, eq.mark("gp1"))
        m_rs[ch] = eq.op(G, lambda g, ch=ch: g.collective_compute(
            "ReduceScatter", ALU.add, replica_groups=[list(range(NCORES))],
            ins=[ps_d.ap()[32 * ch:32 * (ch + 1), :, :, :].opt()],
            outs=[psred_d.ap()[4 * ch:4 * (ch + 1), :, :, :].opt()]), key="gcc")

    m_sct = {}
    m_scb = {}

    def emit_y_group(ch, gi, wbuf, m_wgate):
        if True:
            grp = 2 * ch + gi
            ug = grp % 2
            if gi == 0:
                eq.wait(G, m_rs[ch])
            if grp >= 2:
                eq.wait(G, m_scb[grp - 2])
            m_scr = eq.op(G, lambda g, grp=grp, ug=ug: g.dma_start(
                out=sc_stage[:, ug, :], in_=psred_rows[grp * 128:(grp + 1) * 128, :]),
                inc=16, key="gr%d" % ug)
            eq.wait(S, m_scr)
            if grp >= 2:
                eq.wait(S, m_sct.get(grp - 2) or m_sct_prev[grp - 2])
            m_scb[grp] = eq.op(S, lambda s, ug=ug: s.copy(scb_stage[:, ug, :], sc_stage[:, ug, :]))
            eq.wait(Y, m_scb[grp])
            if grp >= 2:
                eq.wait(Y, m_ymm.get(2 * (grp - 2) + 1))
            m_sct[grp] = eq.op(Y, lambda y, ug=ug: y.dma_start(
                out=sct_stage[:, ug, :].rearrange("p (t c) -> p t c", t=8),
                in_=scb_stage[:, ug, :].rearrange("p (t c) -> p t c", t=8),
                transpose=True), inc=16, key="yt")
            for k in (0, 1):
                n = 2 * grp + k
                nk = 2 * gi + k
                wv, wkey, widx = wbuf[n % len(wbuf)]
                WE = (Y if n % 2 == 0 else G) if USE_WEXPB else G
                eq.wait(WE, m_wgate)
                eq.wait(WE, eq.mark("gwc"))
                if ch == 0 and n >= 1:
                    eq.wait(WE, m_ymm[n - 1])
                if ch == 1 and n - 4 >= len(wbuf):
                    eq.wait(WE, m_ymm[n - len(wbuf)])
                if USE_WEXPB:
                    m_wx = eq.op(WE, lambda g, n=n, wv=wv: g.dma_start(
                        out=wv, in_=wexpb_d[n].rearrange("(c p) o -> p c o", p=128)),
                        inc=16, key=wkey + ("y" if n % 2 == 0 else "g"))
                else:
                    m_wx = eq.op(G, lambda g, n=n, wv=wv: g.dma_start(
                        out=wv, in_=wexp_d[n].rearrange("(c p) o -> p c o", p=128)),
                        inc=16, key=wkey)
                bbase, bh = (RS0, RS1, RS2, RS0)[nk], (0, 0, 0, 1)[nk]
                eq.wait(T, m_bex[ch])
                eq.wait(T, m_sct[grp])
                eq.wait(T, m_wx)
                t_acquire(4)
                t_acquire(5)
                colb = k * 64
                for half, bi in ((0, 4), (1, 5)):
                    for c in range(8):
                        eq.silent(T, lambda t, wv=wv, c=c, half=half, bi=bi, ug=ug, colb=colb: t.matmul(
                            pb[bi][0:64, :], sct_stage[:, ug, c * 128 + colb:c * 128 + colb + 64],
                            wv[:, c, hs(half)], start=(c == 0), stop=False, skip_group_check=True))
                    eq.op(T, lambda t, half=half, bi=bi, bbase=bbase, bh=bh: t.matmul(
                        pb[bi][0:64, :], onesb_at(bbase, 64),
                        rowsb[bbase:bbase + 1, bh * 1024 + half * H:bh * 1024 + (half + 1) * H],
                        start=False, stop=True, skip_group_check=True))
                m_ymm[n] = eq.mark(T)
                eq.wait(V, m_ymm[n])
                un = n % 2
                if n >= 2:
                    eq.wait(V, m_ydma[n - 2])
                for half, bi in ((0, 4), (1, 5)):
                    eq.op(V, lambda v, un=un, half=half, bi=bi: v.tensor_copy(
                        y_stage[0:64, un, hs(half)], pb[bi][0:64, :]))
                m_ye = eq.mark(V)
                bank_free[4] = bank_free[5] = m_ye
                eq.wait(G, m_ye)
                m_ydma[n] = eq.op(G, lambda g, n=n, un=un: g.dma_start(
                    out=yin_d.ap()[n * 64:(n + 1) * 64, :], in_=y_stage[0:64, un, :]),
                    inc=16, key="gy%d" % un)
    def emit_agy(ch):
        eq.wait(G, eq.mark("gy0"))
        eq.wait(G, eq.mark("gy1"))
        m_agy[ch] = eq.op(G, lambda g, ch=ch: g.collective_compute(
            "AllGather", ALU.bypass, replica_groups=[list(range(NCORES))],
            ins=[yin_d.ap()[256 * ch:256 * (ch + 1), :].opt()],
            outs=[yag_d.ap()[2048 * ch:2048 * (ch + 1), :].opt()]), key="gcc")

    for st in range(8):
        emit_slots_st(st)
    emit_rs(0)
    eq.wait(G, m_ffni_T)
    eq.wait(G, m_ffni_V)
    eq.wait(G, eq.mark("yt"))
    eq.wait(G, eq.mark("st"))
    wbuf0 = [(wexp0, "gx0", None)]
    m_bex = [None, None]

    def emit_bexp_rows(ch, gate):
        eq.wait(G, gate)
        mm = None
        for nk in range(4):
            bbase, bh = (RS0, RS1, RS2, RS0)[nk], (0, 0, 0, 1)[nk]
            mm = eq.op(G, lambda g, ch=ch, nk=nk, bbase=bbase, bh=bh: g.dma_start(
                out=rowsb[bbase:bbase + 1, bh * 1024:(bh + 1) * 1024],
                in_=brows_d[B_BEXP + 4 * ch + nk:B_BEXP + 4 * ch + nk + 1, :]), inc=16, key="grow")
        m_bex[ch] = mm

    emit_bexp_rows(0, m_ffni_V)
    for st in range(8, 16):
        emit_slots_st(st)
        if st == 11:
            emit_y_group(0, 0, wbuf0, m_ffni_V)
        if st == 15:
            emit_y_group(0, 1, wbuf0, m_ffni_V)
    emit_rs(1)
    emit_agy(0)
    m_slots_T = eq.mark(T)
    eq.wait(G, m_slots_T)
    wbuf1 = [(wexp[:, 0, :, :], "gx0", None), (wexp[:, 1, :, :], "gx1", None),
             (_bv(ov, OFF_H, 16384, BF16).rearrange("p (c o) -> p c o", c=8), "gx2", None),
             (_bv(ov, OFF_H + 16384, 16384, BF16).rearrange("p (c o) -> p c o", c=8), "gx3", None)]
    eq.wait(G, eq.mark(T))
    emit_bexp_rows(1, m_slots_T)
    emit_y_group(1, 0, wbuf1, m_slots_T)
    emit_y_group(1, 1, wbuf1, m_slots_T)
    emit_agy(1)

    m_yld = {}
    for ch in range(2):
        yag_r = yag_d.ap()[2048 * ch:2048 * (ch + 1), :].rearrange("(n b p) d -> n b p d", n=32, b=2)
        eq.wait(G, m_agy[ch])
        if ch == 1:
            eq.wait(Y, m_agy[ch])
        for st in range(8 * ch, 8 * ch + 8):
            for b in range(2):
                e_, k_ = ((G, "gyy") if (ch == 0 or (2 * st + b) % 2 == 0) else (Y, "gyyY"))
                m_yld[(st, b)] = eq.op(e_, lambda g, st=st, b=b, ch=ch, yag_r=yag_r: g.dma_start(
                    out=yy[:, st, b, :], in_=yag_r[4 * (st - 8 * ch):4 * (st - 8 * ch + 1), b, :, :]),
                    inc=16, key=k_)
    m_outdma = None
    m_fin_prev = None
    m_hld_prev = None
    for b in range(2):
        for mj in range(NT):
            eq.wait(S, m_fin_prev)
            m_hld = eq.op(S, lambda s, b=b, mj=mj: s.dma_start(
                out=h32_stage, in_=h32_d[b * 1024 + mj * 128:b * 1024 + (mj + 1) * 128, :]), inc=16, key="sdh")
            oa = 2 * ((b * NT + mj) % 2)
            t_acquire(oa)
            t_acquire(oa + 1)
            eq.wait(T, eq.mark("gyy"))
            eq.wait(T, eq.mark("gyyY"))
            eq.wait(T, eq.mark("yt"))
            eq.wait(T, eq.mark("st"))
            for st in range(16):
                emitfn = eq.op if st == 15 else eq.silent
                for half, bi in ((0, oa), (1, oa + 1)):
                    emitfn(T, lambda t, st=st, b=b, mj=mj, half=half, bi=bi: t.matmul(
                        pb[bi][:, :], combT[:, st, b, mj, :], yy[:, st, b, hs(half)],
                        start=(st == 0), stop=(st == 15), skip_group_check=True))
            m_mm = eq.mark(T)
            eq.wait(V, m_mm)
            eq.wait(V, m_hld)
            for half, bi in ((0, oa), (1, oa + 1)):
                eq.op(V, lambda v, half=half, bi=bi: v.tensor_add(
                    sq[:, hs(half)], pb[bi][:, :], h32_stage[:, hs(half)]))
            m_r2 = eq.mark(V)
            bank_free[oa] = bank_free[oa + 1] = m_r2
            eq.wait(S, m_r2)
            m_r2s = eq.op(S, lambda s: s.activation(h32_stage, sq, AF.Square,
                                                    accum_out=stats[:, C_VARS:C_VARS + 1]))
            eq.op(V, lambda v: v.reduce_sum(stats[:, C_SUM:C_SUM + 1], sq, axis=AX.X))
            eq.wait(V, eq.mark("V"))
            eq.op(V, lambda v: v.tensor_scalar_mul(stats[:, C_MEAN:C_MEAN + 1], stats[:, C_SUM:C_SUM + 1], 1.0 / 1024))
            eq.wait(V, eq.mark("V"))
            eq.wait(V, m_r2s)
            eq.op(V, lambda v: v.tensor_scalar(
                stats[:, C_LNV:C_LNV + 1], stats[:, C_MEAN:C_MEAN + 1], stats[:, C_MEAN:C_MEAN + 1], None,
                ALU.mult))
            eq.wait(V, eq.mark("V"))
            eq.op(V, lambda v: v.tensor_scalar(
                stats[:, C_VEPS:C_VEPS + 1], stats[:, C_VARS:C_VARS + 1], 1.0 / 1024, EPS_LN, ALU.mult, ALU.add))
            eq.wait(V, eq.mark("V"))
            m_v = eq.op(V, lambda v: v.tensor_sub(
                stats[:, C_VEPS:C_VEPS + 1], stats[:, C_VEPS:C_VEPS + 1], stats[:, C_LNV:C_LNV + 1]))
            eq.wait(S, m_v)
            eq.op(S, lambda s: s.activation(stats[:, C_LNV:C_LNV + 1], stats[:, C_VEPS:C_VEPS + 1], AF.Ln))
            eq.wait(S, eq.mark("S"))
            m_rstd = eq.op(S, lambda s: s.activation(stats[:, C_RSTD:C_RSTD + 1], stats[:, C_LNV:C_LNV + 1],
                                                     AF.Exp, scale=-0.5))
            eq.wait(V, m_rstd)
            eq.wait(V, m_outdma)
            pbuf = (b * NT + mj) % 2
            m_fin = eq.op(V, lambda v, pbuf=pbuf: v.tensor_scalar(
                x_stage[:, pbuf, :], sq, stats[:, C_MEAN:C_MEAN + 1],
                stats[:, C_RSTD:C_RSTD + 1], ALU.subtract, ALU.mult))
            m_fin_prev = m_fin
            eq.wait(S, m_fin)
            m_outdma = eq.op(S, lambda s, b=b, mj=mj, pbuf=pbuf: s.dma_start(
                out=out_d[b, mj * 128:(mj + 1) * 128, :], in_=x_stage[:, pbuf, :]), inc=16, key="sdo")

    _finalize(nc, eq, ctx, dbg, debug, h32_d=h32_d, rows=rows, d0=d0, zq=zq, kk=kk)
    return nc


def _finalize(nc, eq, ctx, dbg, debug, h32_d=None, rows=None, d0=None, zq=None, kk=None):
    G = "G"
    if debug:
        for k in list(eq.cnt):
            eq.wait(G, eq.mark(k))
        if "dbg_h" in dbg and h32_d is not None:
            eq.op(G, lambda g: g.dma_start(out=dbg["dbg_h"], in_=h32_d), inc=16, key="gdbg")
        if "dbg_rows" in dbg and rows is not None:
            eq.op(G, lambda g: g.dma_start(out=dbg["dbg_rows"][0:16, :], in_=rows[0:16, :]), inc=16, key="gdbg")
        if "dbg_d0" in dbg and d0 is not None:
            eq.op(G, lambda g: g.dma_start(
                out=dbg["dbg_d0"].rearrange("(j p) s -> p j s", p=128), in_=d0), inc=16, key="gdbg")
        if "dbg_q" in dbg and zq is not None:
            eq.op(G, lambda g: g.dma_start(
                out=dbg["dbg_q"][:, 0:1024].rearrange("(j p) s -> p j s", p=128), in_=zq), inc=16, key="gdbg")
        if "dbg_oar" in dbg and kk is not None:
            eq.op(G, lambda g: g.dma_start(
                out=dbg["dbg_oar"][:, 0:1024].rearrange("(j p) s -> p j s", p=128), in_=kk), inc=16, key="gdbg")
    finals = {k: eq.mark(k) for k in list(eq.cnt)}
    for e in eq.ENGS:
        for k, mk in finals.items():
            if k != e:
                eq.wait(e, mk)

    block = ctx.enter_context(nc.Block())

    @block.tensor
    def _(t):
        for fn in eq.q["T"]:
            fn(t)

    @block.vector
    def _(v):
        for fn in eq.q["V"]:
            fn(v)

    @block.scalar
    def _(s):
        for fn in eq.q["S"]:
            fn(s)

    @block.gpsimd
    def _(g):
        for fn in eq.q["G"]:
            fn(g)

    @block.sync
    def _(y):
        for fn in eq.q["Y"]:
            fn(y)

    ctx.close()


def _host_inputs(inputs):
    x = np.ascontiguousarray(np.asarray(inputs["L0_attn_x"], dtype=np.float32))
    Wq1 = np.asarray(inputs["L0_attn_Wq1"], np.float32)
    Wk1 = np.asarray(inputs["L0_attn_Wk1"], np.float32)
    Wk2 = np.asarray(inputs["L0_attn_Wk2"], np.float32)
    Wproj = np.asarray(inputs["L0_attn_Wproj"], np.float32)
    bq1 = np.asarray(inputs["L0_attn_bq1"], np.float32)
    bk1 = np.asarray(inputs["L0_attn_bk1"], np.float32)
    bk2 = np.asarray(inputs["L0_attn_bk2"], np.float32)
    bproj = np.asarray(inputs["L0_attn_bproj"], np.float32)
    phi = np.ascontiguousarray(np.asarray(inputs["L0_ffn_phi"], np.float32).reshape(1024, 2048))
    W = np.asarray(inputs["L0_ffn_W"], np.float32)
    bexp = np.asarray(inputs["L0_ffn_b"], np.float32)

    wcat = np.ascontiguousarray(np.concatenate([Wq1.T, Wk1.T, Wk2.T], axis=1))
    wproj = np.ascontiguousarray(Wproj.T)
    tri = np.ascontiguousarray(np.concatenate(
        [np.triu(np.ones((128, 128), np.float32)), np.ones((128, 128), np.float32)], axis=1))

    in_maps = []
    for c in range(NCORES):
        lo = c * 1024
        halo = np.zeros((2, 1, 1024), np.float32) if c == 0 else x[:, lo - 1:lo, :]
        x_pad = np.ascontiguousarray(np.concatenate([halo, x[:, lo:lo + 1024, :]], axis=1))
        brows = np.zeros((12, 1024), np.float32)
        brows[B_BQ1] = bq1
        brows[B_BK1] = bk1
        brows[B_BK2] = bk2
        brows[B_BPROJ] = bproj
        eorder = list(range(4 * c, 4 * c + 4)) + list(range(32 + 4 * c, 32 + 4 * c + 4))
        brows[B_BEXP:B_BEXP + 8] = bexp[eorder]
        pmask = np.zeros((96, 132), np.float32)
        for r in range(8):
            if r < c:
                pmask[2 * r, 0] = 1.0
                pmask[2 * r + 1, 1] = 1.0
                pmask[r, 2] = 1.0
        pmask[0, 3] = 0.0 if c == 0 else 1.0
        pmask[:, 4:132] = 1.0
        in_maps.append({
            "x": x_pad, "wcat": wcat, "wproj": wproj, "phi": phi,
            "wexp": np.ascontiguousarray(W[eorder]),
            "brows": brows, "tri": tri, "pmask": pmask,
        })
    return in_maps


_NC_CACHE = {}


def kernel(**inputs):
    in_maps = _host_inputs(inputs)
    if "full" not in _NC_CACHE:
        _NC_CACHE["full"] = build_nc()
    nc = _NC_CACHE["full"]
    res = run_bass_kernel_spmd(nc, in_maps, core_ids=list(range(NCORES)))
    out = np.empty((2, 8192, 1024), np.float32)
    for c in range(NCORES):
        out[:, c * 1024:(c + 1) * 1024, :] = res.results[c]["out"]
    return out


# version 28
# speedup vs baseline: 1.0634x; 1.0558x over previous
import contextlib
import numpy as np

import concourse.bass as bass
import concourse.mybir as mybir
from concourse.bass_utils import run_bass_kernel_spmd

F32 = mybir.dt.float32
BF16 = mybir.dt.bfloat16
U8 = mybir.dt.uint8
AF = mybir.ActivationFunctionType
ALU = mybir.AluOpType
AX = mybir.AxisListType

NCORES = 8
NT = 8
H = 512
SIGK = 6.25e-4
EPS_LN = 1e-5

RS0, RS1, RS2 = 0, 32, 64
B_BQ1, B_BK1, B_BK2, B_BPROJ = 0, 1, 2, 3
B_BEXP = 4

C_SUM, C_MEAN, C_VARS, C_VEPS, C_LNV, C_RSTD = 0, 1, 2, 3, 4, 5
C_ESUM4, C_ESUM, C_ERCP = 6, 14, 15

OV_BYTES = 180224
USE_WEXPB = False
USE_WARMUP = True
USE_EARLY_PHI = True
USE_EARLY_HT = False
OFF_WCAT = 0
OFF_WPROJ = 49152
OFF_ZQ = 65536
OFF_ZP2 = 81920
OFF_K2 = 98304
OFF_K = 114688
OFF_KV = 131072
OFF_H = 147456
OFF_PHI = 0
OFF_FSTG = 32768
OFF_COMBT = 49152
OFF_D0 = 114688
OFF_WEXP = 114688
OFF_Y = 114688


class EQ:
    ENGS = ("T", "V", "S", "G", "Y")

    def __init__(self, nc, ctx):
        self.nc = nc
        self.ctx = ctx
        self.q = {e: [] for e in self.ENGS}
        self.sems = {}
        self.cnt = {}
        self.last_wait = {e: {} for e in self.ENGS}
        for e in self.ENGS:
            self._mk(e)

    def _mk(self, name):
        if name not in self.sems:
            self.sems[name] = self.ctx.enter_context(self.nc.semaphore("sem_" + name))
            self.cnt[name] = 0

    def op(self, e, fn, inc=1, key=None):
        key = key or e
        self._mk(key)
        sem = self.sems[key]
        self.cnt[key] += inc
        val = self.cnt[key]
        selfw = (e in ("V", "S") and inc == 1 and key == e)
        def run(eng, fn=fn, sem=sem, inc=inc, val=val, selfw=selfw):
            fn(eng).then_inc(sem, inc)
            if selfw:
                eng.wait_ge(sem, val)
        self.q[e].append(run)
        return (key, val)

    def silent(self, e, fn):
        self.q[e].append(fn)

    def wait(self, e, mark):
        if mark is None:
            return
        key, val = mark
        if val is None or val <= 0:
            return
        if self.last_wait[e].get(key, 0) >= val:
            return
        self.last_wait[e][key] = val
        sem = self.sems[key]
        self.q[e].append(lambda eng, sem=sem, val=val: eng.wait_ge(sem, val))

    def mark(self, key):
        self._mk(key)
        return (key, self.cnt[key])


def _bv(ov, off, nbytes, dtype):
    return ov.ap()[:, off:off + nbytes].bitcast(dtype)


def build_nc(upto="full", debug=False):
    nc = bass.Bass("TRN2", target_bir_lowering=False)

    x_d = nc.dram_tensor("x", [2, 1025, 1024], F32, kind="ExternalInput").ap()
    wcat_d = nc.dram_tensor("wcat", [1024, 3072], F32, kind="ExternalInput").ap()
    wproj_d = nc.dram_tensor("wproj", [1024, 1024], F32, kind="ExternalInput").ap()
    phi_d = nc.dram_tensor("phi", [1024, 2048], F32, kind="ExternalInput").ap()
    wexp_d = nc.dram_tensor("wexp", [8, 1024, 1024], F32, kind="ExternalInput").ap()
    brows_d = nc.dram_tensor("brows", [12, 1024], F32, kind="ExternalInput").ap()
    tri_d = nc.dram_tensor("tri", [128, 256], F32, kind="ExternalInput").ap()
    pmask_d = nc.dram_tensor("pmask", [96, 132], F32, kind="ExternalInput").ap()
    out_d = nc.dram_tensor("out", [2, 1024, 1024], F32, kind="ExternalOutput").ap()

    h32_d = nc.dram_tensor("h32", [2048, 1024], F32, kind="Internal").ap()
    zp2_d = nc.dram_tensor("zp2b", [1025, 1024], BF16, kind="Internal").ap()
    q_d = nc.dram_tensor("qb", [1025, 1024], BF16, kind="Internal").ap()
    oar_d = nc.dram_tensor("oarb", [1025, 1024], BF16, kind="Internal").ap()
    ps_d = nc.dram_tensor("ps", [64, 2, 32, 1024], BF16, kind="Internal")
    psred_d = nc.dram_tensor("psred", [8, 2, 32, 1024], BF16, kind="Internal")
    yin_d = nc.dram_tensor("yin", [512, 1024], BF16, kind="Internal")
    yag_d = nc.dram_tensor("yag", [4096, 1024], BF16, kind="Internal")
    wexpb_d = nc.dram_tensor("wexpb", [8, 1024, 1024], BF16, kind="Internal").ap()
    wu_in = nc.dram_tensor("wu_in", [1, 128], F32, kind="Internal")
    wu_out = nc.dram_tensor("wu_out", [8, 128], F32, kind="Internal")
    agt_in = [nc.dram_tensor("agt_in%d" % b, [2, 1024], F32, kind="Internal") for b in range(2)]
    agt_out = [nc.dram_tensor("agt_out%d" % b, [16, 1024], F32, kind="Internal") for b in range(2)]
    ags_in = [nc.dram_tensor("ags_in%d" % b, [1, 1024], F32, kind="Internal") for b in range(2)]
    ags_out = [nc.dram_tensor("ags_out%d" % b, [8, 1024], F32, kind="Internal") for b in range(2)]

    dbg = {}
    if debug:
        for name, shape in [("dbg_h", [2048, 1024]), ("dbg_rows", [96, 1024]),
                            ("dbg_q", [1024, 2048]), ("dbg_oar", [1024, 2048]),
                            ("dbg_d0", [1024, 2048]), ("dbg_slots", [128, 2048])]:
            dbg[name] = nc.dram_tensor(name, shape, F32, kind="ExternalOutput").ap()

    ctx = contextlib.ExitStack()
    sb = lambda n, s, d: ctx.enter_context(nc.sbuf_tensor(n, s, d))

    ov = ctx.enter_context(nc.sbuf_tensor("ov", [128, OV_BYTES], U8))
    rows = sb("rows_s", [96, 1024], F32).ap()
    pmask2 = sb("pmask2_s", [96, 132], F32).ap()
    pmask = pmask2[0:16, 0:4]
    onesr = pmask2[:, 4:132]
    onesrb = sb("onesrb_s", [96, 128], BF16).ap()
    stats = sb("stats", [128, 18], F32).ap()
    tri2 = sb("tri_s", [128, 256], BF16).ap()
    tri_sb = tri2[:, 0:128]
    allon = tri2[:, 128:256]
    onec_sb = allon[:, 0:1]
    x_stage = sb("x_stage", [128, 2048], F32).ap().rearrange("p (u d) -> p u d", u=2)
    xb2 = sb("xb_stage", [128, 2048], BF16).ap().rearrange("p (u d) -> p u d", u=2)
    xt_stage2 = sb("xt_stage", [128, 2, 1024], BF16).ap()
    xt_stage = xt_stage2[:, 0, :]
    sq = sb("sq", [128, 1024], F32).ap()
    h32_stage = sb("h32_stage", [128, 1024], F32).ap()
    haloT = sb("haloT", [128, 256], BF16).ap().rearrange("p (b c r) -> p b c r", b=2, c=8)

    wcat = _bv(ov, OFF_WCAT, 49152, BF16).rearrange("p (c o) -> p c o", c=8)
    wproj = _bv(ov, OFF_WPROJ, 16384, BF16).rearrange("p (c o) -> p c o", c=8)
    zq = _bv(ov, OFF_ZQ, 16384, BF16).rearrange("p (j d) -> p j d", j=8)
    zp2 = _bv(ov, OFF_ZP2, 16384, BF16).rearrange("p (j d) -> p j d", j=8)
    k2 = _bv(ov, OFF_K2, 16384, BF16).rearrange("p (j d) -> p j d", j=8)
    kk = _bv(ov, OFF_K, 16384, BF16).rearrange("p (j d) -> p j d", j=8)
    kv = _bv(ov, OFF_KV, 16384, BF16).rearrange("p (j d) -> p j d", j=8)
    hh = _bv(ov, OFF_H, 32768, BF16).rearrange("p (t d) -> p t d", t=16)
    phi = _bv(ov, OFF_PHI, 32768, BF16).rearrange("p (c s) -> p c s", c=8)
    combT = _bv(ov, OFF_COMBT, 65536, BF16).rearrange("p (st b j m) -> p st b j m", st=16, b=2, j=8)
    d0 = _bv(ov, OFF_D0, 32768, BF16).rearrange("p (j s) -> p j s", j=8)
    wexp = _bv(ov, OFF_WEXP, 32768, BF16).rearrange("p (u c o) -> p u c o", u=2, c=8)
    yy = _bv(ov, OFF_Y, 65536, BF16).rearrange("p (st b e) -> p st b e", st=16, b=2)
    ht_stage = _bv(ov, OFF_FSTG, 4096, BF16).rearrange("p (u d) -> p u d", u=2)
    l0_sb = _bv(ov, OFF_FSTG + 4096, 8192, F32)
    e2_sb = _bv(ov, OFF_FSTG + 12288, 4096, BF16)
    e2x = _bv(ov, OFF_FSTG + 4096, 8192, BF16).rearrange("p (u d) -> p u d", u=2)
    rcpf = _bv(ov, OFF_FSTG + 12288, 4096, F32)
    rcpb = sq.bitcast(BF16)
    sl_stage = _bv(ov, OFF_PHI, 4096, BF16).rearrange("p (u d) -> p u d", u=2)
    d1_stage = _bv(ov, OFF_PHI + 8192, 2048, BF16)
    sc_stage = _bv(ov, OFF_PHI + 10240, 8192, F32).rearrange("p (u d) -> p u d", u=2)
    scb_stage = _bv(ov, OFF_PHI + 18432, 4096, BF16).rearrange("p (u d) -> p u d", u=2)
    sct_stage = _bv(ov, OFF_PHI + 22528, 4096, BF16).rearrange("p (u d) -> p u d", u=2)
    y_stage = _bv(ov, OFF_PHI + 26624, 4096, BF16).rearrange("p (u d) -> p u d", u=2)

    pb = [ctx.enter_context(nc.psum_tensor("pb%d" % i, [128, H], F32)) for i in range(8)]

    eq = EQ(nc, ctx)
    T, V, S, G, Y = "T", "V", "S", "G", "Y"
    def ones_at(base, n=128):
        return onesr[base:base + 1, 0:n]

    def onesb_at(base, n=128):
        return onesrb[base:base + 1, 0:n]

    rowsb = rows.bitcast(BF16)

    def biasrow(base, half):
        return rowsb[base:base + 1, half * H:(half + 1) * H]

    sqb = sq.bitcast(BF16)
    rowz_qh = sqb[0:1, 1024:2048]
    rowz_oarh = sqb[0:1, 0:1024]
    rowz_sh = rowz_oarh
    agst = rows[0:16, :]
    bank_free = [None] * 8

    def t_acquire(i):
        eq.wait(T, bank_free[i])

    def hs(half):
        return slice(half * H, (half + 1) * H)

    xbuf_free = [None, None]
    xb_cons = [[], []]

    def xb_wait(buf, eng=S):
        for mk in xb_cons[buf]:
            eq.wait(eng, mk)
        del xb_cons[buf][:]

    def load_x_tile(b, j, buf):
        eq.wait(Y, xbuf_free[buf])
        return eq.op(Y, lambda y, b=b, j=j, buf=buf: y.dma_start(
            out=x_stage[:, buf, :], in_=x_d[b, 1 + j * 128:1 + (j + 1) * 128, :]),
            inc=16, key="yx%d" % buf)

    if USE_WARMUP:
        m_wu = eq.op(G, lambda g: g.dma_start(out=wu_in.ap(), in_=pmask_d[0:1, 0:128]), inc=16, key="gwc")
        eq.wait(G, m_wu)
        eq.op(G, lambda g: g.collective_compute(
            "AllGather", ALU.bypass, replica_groups=[list(range(NCORES))],
            ins=[wu_in.ap().opt()], outs=[wu_out.ap().opt()]), key="gcc")
    eq.op(G, lambda g: g.dma_start(out=onesrb, in_=pmask_d[:, 4:132]), inc=16, key="gw")
    m_tri = eq.op(G, lambda g: g.dma_start(out=tri2, in_=tri_d), inc=16, key="gw")
    m_onec = m_tri
    m_pmask = eq.op(G, lambda g: g.dma_start(out=pmask2, in_=pmask_d), inc=16, key="gw")
    m_wcat = eq.op(G, lambda g: g.dma_start(
        out=wcat, in_=wcat_d.rearrange("(c p) o -> p c o", p=128)), inc=16, key="gw")
    m_wproj = eq.op(G, lambda g: g.dma_start(
        out=wproj, in_=wproj_d.rearrange("(c p) o -> p c o", p=128)), inc=16, key="gw")
    m_gw0 = eq.mark("gw")
    m_rows = m_onesr = m_tri = m_onec = m_pmask = m_wcat = m_wproj = m_gw0

    m_ms = eq.op(V, lambda v: v.memzero(xb2[:, 0, :]))
    eq.wait(G, m_ms)
    m_hl = [eq.op(G, lambda g, b=b: g.dma_start(
        out=xb2[b * 32:b * 32 + 1, 0, :], in_=x_d[b, 0:1, :]), inc=16, key="ghl") for b in range(2)]
    eq.wait(Y, m_hl[1])
    m_htp = [eq.op(Y, lambda y, b=b: y.dma_start(
        out=haloT[:, b, :, :],
        in_=xb2[b * 32:b * 32 + 16, 0, :].rearrange("p (c f) -> p c f", c=8),
        transpose=True), inc=16, key="yt") for b in range(2)]
    xb_cons[0].append(m_htp[1])

    def emit_bias_slots(b):
        eq.wait(G, eq.mark(T))
        m = []
        for bsrc, rdst in ((B_BQ1, RS0), (B_BK1, RS1), (B_BK2, RS2)):
            m.append(eq.op(G, lambda g, bsrc=bsrc, rdst=rdst: g.dma_start(
                out=rowsb[rdst:rdst + 1, 0:1024], in_=brows_d[bsrc:bsrc + 1, :]), inc=16, key="grow"))
        return m[-1]

    def emit_halo_proj(b, m_bias):
        eq.wait(T, m_htp[1])
        eq.wait(T, m_wcat)
        eq.wait(T, m_bias)
        for pi, (o_base, r_b) in ((0, (0, RS0)), (1, (2048, RS2))):
            for half in range(2):
                bi = 4 + 2 * pi + half
                t_acquire(bi)
                for c in range(8):
                    eq.silent(T, lambda t, b=b, c=c, o=o_base, half=half, bi=bi: t.matmul(
                        pb[bi][0:1, :], haloT[:, b, c, 0:1],
                        wcat[:, c, o + half * H:o + half * H + H],
                        start=(c == 0), stop=False, skip_group_check=True))
                eq.op(T, lambda t, r_b=r_b, half=half, bi=bi: t.matmul(
                    pb[bi][0:1, :], onesb_at(r_b, 1), biasrow(r_b, half),
                    start=False, stop=True, skip_group_check=True))
        m_mm = eq.mark(T)
        eq.wait(V, m_mm)
        for half in range(2):
            eq.op(V, lambda v, half=half: v.tensor_copy(rowz_qh[:, hs(half)], pb[4 + half][0:1, :]))
            eq.op(V, lambda v, half=half: v.tensor_copy(k2[0:1, 7, hs(half)], pb[6 + half][0:1, :]))
        m_e = eq.mark(V)
        for i in range(4, 8):
            bank_free[i] = m_e
        return m_e

    def emit_pass1a(b, m_hprev):
        for i in range(4, 8):
            t_acquire(i)
        eq.wait(V, m_hprev)
        xm = [load_x_tile(b, 0, 0)]
        m_cast = [None] * NT
        m_xt = [None] * NT
        m_kvv = [None] * NT
        m_projlast = None
        m_proj2last = None
        for j in range(NT):
            buf = j % 2
            eq.wait(V, xm[j])
            xb_wait(buf, V)
            m_cast[j] = eq.op(V, lambda v, buf=buf: v.tensor_copy(xb2[:, buf, :], x_stage[:, buf, :]))
            if j + 1 < NT:
                xbuf_free[buf] = m_cast[j]
                xm.append(load_x_tile(b, j + 1, 1 - buf))
            eq.wait(Y, m_cast[j])
            if j >= 2:
                eq.wait(Y, m_proj2last)
            xu = j % 2
            m_xt[j] = eq.op(Y, lambda y, xu=xu, buf=buf: y.dma_start(
                out=xt_stage2[:, xu, :].rearrange("p (t c) -> p t c", t=8),
                in_=xb2[:, buf, :].rearrange("p (t c) -> p t c", t=8),
                transpose=True), inc=16, key="yt%d" % xu)
            xb_cons[buf].append(m_xt[j])
            eq.wait(T, m_xt[j])
            for pi in range(3):
                for half in range(2):
                    bi = (2 * pi + half) % 4
                    t_acquire(bi)
                    o0 = pi * 1024 + half * H
                    for c in range(8):
                        eq.silent(T, lambda t, c=c, o0=o0, bi=bi, xu=xu: t.matmul(
                            pb[bi][:, :], xt_stage2[:, xu, c * 128:(c + 1) * 128],
                            wcat[:, c, o0:o0 + H], start=(c == 0), stop=False,
                            skip_group_check=True))
                    r_b = (RS0, RS1, RS2)[pi]
                    m_mm = eq.op(T, lambda t, r_b=r_b, half=half, bi=bi: t.matmul(
                        pb[bi][:, :], onesb_at(r_b), biasrow(r_b, half),
                        start=False, stop=True, skip_group_check=True))
                    eq.wait(S, m_mm)
                    if pi == 0:
                        fn = lambda s, j=j, half=half, bi=bi: s.copy(zq[:, j, hs(half)], pb[bi][:, :])
                    elif pi == 1:
                        fn = lambda s, j=j, half=half, bi=bi: s.activation(
                            kk[:, j, hs(half)], pb[bi][:, :], AF.Exp)
                    else:
                        fn = lambda s, j=j, half=half, bi=bi: s.copy(zp2[:, j, hs(half)], pb[bi][:, :])
                    m_ev = eq.op(S, fn)
                    bank_free[bi] = m_ev
                    if pi == 2 and half == 1:
                        eq.wait(G, m_ev)
                        eq.op(G, lambda g, j=j: g.dma_start(
                            out=zp2_d[1 + j * 128:1 + (j + 1) * 128, :], in_=zp2[:, j, :]),
                            inc=16, key="gsh")
                if pi == 1:
                    eq.wait(V, eq.mark(S))
                    m_kvv[j] = eq.op(V, lambda v, j=j, buf=buf: v.tensor_mul(kv[:, j, :], kk[:, j, :], xb2[:, buf, :]))
                    xb_cons[buf].append(m_kvv[j])
            m_proj2last = m_projlast
            m_projlast = eq.mark(T)
        eq.wait(T, m_onec)
        for j in range(NT):
            eq.wait(T, m_kvv[j])
            for half in range(2):
                eq.op(T, lambda t, j=j, half=half: t.matmul(
                    pb[4 + half][0:1, :], onec_sb, kk[:, j, hs(half)],
                    start=(j == 0), stop=(j == NT - 1), skip_group_check=True))
                eq.op(T, lambda t, j=j, half=half: t.matmul(
                    pb[6 + half][0:1, :], onec_sb, kv[:, j, hs(half)],
                    start=(j == 0), stop=(j == NT - 1), skip_group_check=True))
        m_tot = eq.mark(T)
        eq.wait(V, m_tot)
        for half in range(2):
            eq.op(V, lambda v, half=half: v.tensor_copy(rows[0:1, hs(half)], pb[4 + half][0:1, :]))
            eq.op(V, lambda v, half=half: v.tensor_copy(h32_stage[0:1, hs(half)], pb[6 + half][0:1, :]))
        m_ev = eq.mark(V)
        for i in range(4, 8):
            bank_free[i] = m_ev
        eq.wait(G, m_ev)
        eq.op(G, lambda g, b=b: g.dma_start(out=agt_in[b].ap()[0:1, :], in_=rows[0:1, :]), inc=16, key="gag")
        eq.op(G, lambda g, b=b: g.dma_start(out=agt_in[b].ap()[1:2, :], in_=h32_stage[0:1, :]), inc=16, key="gag")
        eq.wait(G, eq.mark("gag"))
        eq.op(G, lambda g, b=b: g.collective_compute(
            "AllGather", ALU.bypass, replica_groups=[list(range(NCORES))],
            ins=[agt_in[b].ap().opt()], outs=[agt_out[b].ap().opt()]), key="gcc")
        eq.wait(G, eq.mark("gcc"))
        m_rd = eq.op(G, lambda g, b=b: g.dma_start(out=agst, in_=agt_out[b].ap()), inc=16, key="gag")
        return m_rd

    def emit_shift_zp2(b):
        eq.wait(G, eq.mark(V))
        eq.op(G, lambda g: g.dma_start(out=zp2_d[0:1, :], in_=k2[0:1, 7, :]), inc=16, key="gsh")
        eq.wait(G, eq.mark("gsh"))
        m = eq.op(G, lambda g: g.dma_start(
            out=k2, in_=zp2_d[0:1024, :].rearrange("(j p) d -> p j d", p=128)), inc=16, key="gsh")
        return m

    def emit_sigmoid_batch(b, m_k2rd):
        eq.wait(S, eq.mark(V))
        eq.op(S, lambda s: s.activation(rowz_qh, rowz_qh, AF.Sigmoid))
        for j in range(NT):
            eq.op(S, lambda s, j=j: s.activation(zq[:, j, :], zq[:, j, :], AF.Sigmoid))
        m_sig = eq.mark(S)
        eq.wait(S, m_k2rd)
        for j in range(NT):
            eq.op(S, lambda s, j=j: s.activation(k2[:, j, :], k2[:, j, :], AF.Sigmoid, scale=SIGK))
        m_sigk2 = eq.mark(S)
        eq.wait(G, m_k2rd)
        eq.wait(G, m_sig)
        eq.op(G, lambda g: g.dma_start(
            out=q_d[1:1025, :].rearrange("(j p) d -> p j d", p=128), in_=zq), inc=16, key="gsh")
        m_qw = eq.op(G, lambda g: g.dma_start(out=q_d[0:1, :], in_=rowz_qh), inc=16, key="gsh")
        return m_sig, m_sigk2, m_qw

    def emit_carry_setup(b, m_ag, m_sig):
        eq.wait(T, m_ag)
        eq.wait(T, m_pmask)
        for i in (4, 5, 6, 7):
            t_acquire(i)
        for half in range(2):
            eq.op(T, lambda t, half=half: t.matmul(
                pb[4 + half][0:1, :], pmask[0:16, 0:1], agst[:, hs(half)],
                start=True, stop=True, skip_group_check=True))
            eq.op(T, lambda t, half=half: t.matmul(
                pb[6 + half][0:1, :], pmask[0:16, 1:2], agst[:, hs(half)],
                start=True, stop=True, skip_group_check=True))
        m_mm = eq.mark(T)
        eq.wait(V, m_mm)
        for half in range(2):
            eq.op(V, lambda v, half=half: v.tensor_scalar_add(
                rows[RS0:RS0 + 1, hs(half)], pb[4 + half][0:1, :], 1e-6))
            eq.op(V, lambda v, half=half: v.tensor_copy(h32_stage[0:1, hs(half)], pb[6 + half][0:1, :]))
        m_gkv = eq.mark(V)
        eq.wait(G, m_gkv)
        m_ckv = eq.op(G, lambda g: g.dma_start(out=rows[RS1:RS1 + 1, :], in_=h32_stage[0:1, :]), inc=16, key="grow")
        eq.wait(V, m_ckv)
        eq.op(V, lambda v: v.reciprocal(h32_stage[0:1, :], rows[RS0:RS0 + 1, :]))
        eq.wait(V, eq.mark("V"))
        for half in range(2):
            eq.op(V, lambda v, half=half: v.tensor_mul(h32_stage[0:1, hs(half)], pb[6 + half][0:1, :], h32_stage[0:1, hs(half)]))
        m_sh = eq.op(V, lambda v: v.tensor_copy(rowz_sh, h32_stage[0:1, :]))
        bank_free[4] = bank_free[5] = bank_free[6] = bank_free[7] = m_sh
        eq.wait(V, m_sig)
        m_oarh = eq.op(V, lambda v: v.tensor_mul(rowz_oarh, rowz_oarh, rowz_qh))
        return m_oarh, m_ckv

    def emit_pass1b(b, m_carry, m_ckv, m_sig, m_hprev):
        eq.wait(V, m_hprev)
        eq.wait(T, m_tri)
        eq.wait(T, m_carry)
        eq.wait(T, m_ckv)
        eq.wait(V, m_sig)
        for j in range(NT):
            for qi in range(2):
                srcv = kk if qi == 0 else kv
                carry_r = RS0 if qi == 0 else RS1
                base = 4 * (j % 2)
                b0, b1 = (base, base + 1) if qi == 0 else (base + 2, base + 3)
                t_acquire(b0)
                t_acquire(b1)
                for half, bi in ((0, b0), (1, b1)):
                    eq.silent(T, lambda t, j=j, half=half, srcv=srcv, bi=bi: t.matmul(
                        pb[bi][:, :], tri_sb, srcv[:, j, hs(half)],
                        start=True, stop=False, skip_group_check=True))
                    for i in range(j):
                        eq.silent(T, lambda t, i=i, half=half, srcv=srcv, bi=bi: t.matmul(
                            pb[bi][:, :], allon, srcv[:, i, hs(half)],
                            start=False, stop=False, skip_group_check=True))
                for half, bi in ((0, b0), (1, b1)):
                    eq.op(T, lambda t, half=half, carry_r=carry_r, bi=bi: t.matmul(
                        pb[bi][:, :], ones_at(carry_r), rows[carry_r:carry_r + 1, hs(half)],
                        start=False, stop=True, skip_group_check=True))
                m_mm = eq.mark(T)
                eq.wait(V, m_mm)
                if qi == 0:
                    for half, bi in ((0, b0), (1, b1)):
                        eq.op(V, lambda v, half=half, bi=bi: v.reciprocal(
                            h32_stage[:, hs(half)], pb[bi][:, :]))
                    mk = eq.mark(V)
                    bank_free[b0] = bank_free[b1] = mk
                else:
                    eq.wait(V, eq.mark("V"))
                    for half, bi in ((0, b0), (1, b1)):
                        eq.op(V, lambda v, half=half, bi=bi: v.tensor_mul(
                            xt_stage[:, hs(half)], pb[bi][:, :], h32_stage[:, hs(half)]))
                    mk = eq.mark(V)
                    bank_free[b0] = bank_free[b1] = mk
                    eq.op(V, lambda v, j=j: v.tensor_mul(zp2[:, j, :], xt_stage, zq[:, j, :]))
        return eq.mark(V)

    def emit_shift_oar_q(b, m_p1b):
        eq.wait(G, m_p1b)
        eq.op(G, lambda g: g.dma_start(
            out=oar_d[1:1025, :].rearrange("(j p) d -> p j d", p=128), in_=zp2), inc=16, key="gsh")
        eq.op(G, lambda g: g.dma_start(out=oar_d[0:1, :], in_=rowz_oarh), inc=16, key="gsh")
        eq.wait(G, eq.mark("gsh"))
        m_oar = eq.op(G, lambda g: g.dma_start(
            out=kk, in_=oar_d[0:1024, :].rearrange("(j p) d -> p j d", p=128)), inc=16, key="gsh")
        m_q = eq.op(G, lambda g: g.dma_start(
            out=zq, in_=q_d[0:1024, :].rearrange("(j p) d -> p j d", p=128)), inc=16, key="gsh")
        return m_q, m_q

    def emit_pass2a(b, m_oarsh, m_sigk2):
        t_acquire(6)
        t_acquire(7)
        eq.wait(V, m_oarsh)
        eq.wait(V, m_sigk2)
        xm = [load_x_tile(b, 0, 0)]
        m_k2e = [None] * NT
        m_cast = [None] * NT
        for j in range(NT):
            buf = j % 2
            eq.wait(S, xm[j])
            xb_wait(buf)
            m_cast[j] = eq.op(S, lambda s, buf=buf: s.copy(xb2[:, buf, :], x_stage[:, buf, :]))
            if j + 1 < NT:
                xbuf_free[buf] = m_cast[j]
                xm.append(load_x_tile(b, j + 1, 1 - buf))
            eq.wait(V, m_cast[j])
            eq.op(V, lambda v, j=j, buf=buf: v.tensor_sub(xb2[:, buf, :], xb2[:, buf, :], kk[:, j, :]))
            if j == 0:
                eq.op(V, lambda v: v.tensor_scalar_mul(xb2[0:1, 0, :], xb2[0:1, 0, :], pmask[0:1, 3:4]))
            m_k2e[j] = eq.op(V, lambda v, j=j, buf=buf: v.tensor_mul(kv[:, j, :], k2[:, j, :], xb2[:, buf, :]))
            xb_cons[buf].append(m_k2e[j])
            eq.wait(T, m_k2e[j])
            for half in range(2):
                eq.op(T, lambda t, j=j, half=half: t.matmul(
                    pb[6 + half][0:1, :], onec_sb, kv[:, j, hs(half)],
                    start=(j == 0), stop=(j == NT - 1), skip_group_check=True))
        xbuf_free[1] = m_cast[NT - 1]
        m_tot = eq.mark(T)
        eq.wait(V, m_tot)
        for half in range(2):
            eq.op(V, lambda v, half=half: v.tensor_copy(rows[0:1, hs(half)], pb[6 + half][0:1, :]))
        m_ev = eq.mark(V)
        bank_free[6] = bank_free[7] = m_ev
        eq.wait(G, m_ev)
        eq.op(G, lambda g, b=b: g.dma_start(out=ags_in[b].ap(), in_=rows[0:1, :]), inc=16, key="gag")
        eq.wait(G, eq.mark("gag"))
        eq.op(G, lambda g, b=b: g.collective_compute(
            "AllGather", ALU.bypass, replica_groups=[list(range(NCORES))],
            ins=[ags_in[b].ap().opt()], outs=[ags_out[b].ap().opt()]), key="gcc")
        eq.wait(G, eq.mark("gcc"))
        m_rd = eq.op(G, lambda g, b=b: g.dma_start(out=agst[0:8, :], in_=ags_out[b].ap()), inc=16, key="gag")
        return m_rd

    def emit_wexp_cast(n):
        eq.op(G, lambda g, n=n: g.dma_start(out=wexpb_d[n], in_=wexp_d[n]),
              inc=16, key="gwc")

    def emit_pass2b(b, m_ag, m_qsh):
        eq.wait(T, m_ag)
        t_acquire(4)
        t_acquire(5)
        for half in range(2):
            eq.op(T, lambda t, half=half: t.matmul(
                pb[4 + half][0:1, :], pmask[0:8, 2:3], agst[0:8, hs(half)],
                start=True, stop=True, skip_group_check=True))
        m_mm = eq.mark(T)
        eq.wait(V, m_mm)
        for half in range(2):
            eq.op(V, lambda v, half=half: v.tensor_copy(rows[RS0:RS0 + 1, hs(half)], pb[4 + half][0:1, :]))
        m_cs = eq.mark(V)
        bank_free[4] = bank_free[5] = m_cs
        eq.wait(G, eq.mark(T))
        m_bp = eq.op(G, lambda g: g.dma_start(out=rowsb[RS1:RS1 + 1, 0:1024], in_=brows_d[B_BPROJ:B_BPROJ + 1, :]), inc=16, key="grow")
        eq.wait(T, m_cs)
        eq.wait(V, m_qsh)
        m_h32dma = None
        m_at = [None] * NT
        xm = []
        for j in range(NT + 1):
            if j < NT:
                buf = j % 2
                if j == 0:
                    xm.append(load_x_tile(b, 0, 0))
                    xm.append(load_x_tile(b, 1, 1))
                s2a = 4 * (j % 2)
                t_acquire(s2a)
                t_acquire(s2a + 1)
                for half, bi in ((0, s2a), (1, s2a + 1)):
                    eq.silent(T, lambda t, j=j, half=half, bi=bi: t.matmul(
                        pb[bi][:, :], tri_sb, kv[:, j, hs(half)],
                        start=True, stop=False, skip_group_check=True))
                    for i in range(j):
                        eq.silent(T, lambda t, i=i, half=half, bi=bi: t.matmul(
                            pb[bi][:, :], allon, kv[:, i, hs(half)],
                            start=False, stop=False, skip_group_check=True))
                for half, bi in ((0, s2a), (1, s2a + 1)):
                    eq.op(T, lambda t, half=half, bi=bi: t.matmul(
                        pb[bi][:, :], ones_at(RS0), rows[RS0:RS0 + 1, hs(half)],
                        start=False, stop=True, skip_group_check=True))
                m_mm = eq.mark(T)
                eq.op(V, lambda v, j=j: v.tensor_scalar_mul(sq[1:128, :], zq[0:127, j, :], SIGK)
                      if False else v.tensor_scalar_mul(sq, zq[:, j, :], SIGK))
                if j == 0:
                    eq.op(V, lambda v: v.tensor_scalar_mul(sq[0:1, :], sq[0:1, :], pmask[0:1, 3:4]))
                eq.wait(V, m_mm)
                xu = j % 2
                for half, bi in ((0, s2a), (1, s2a + 1)):
                    eq.op(V, lambda v, half=half, bi=bi, xu=xu: v.tensor_mul(
                        xt_stage2[:, xu, hs(half)], pb[bi][:, :], sq[:, hs(half)]))
                mk = eq.mark(V)
                bank_free[s2a] = bank_free[s2a + 1] = mk
                m_a = eq.op(V, lambda v, j=j, xu=xu: v.tensor_add(kk[:, j, :], xt_stage2[:, xu, :], zp2[:, j, :]))
                eq.wait(Y, m_a)
                if j >= 2:
                    eq.wait(Y, m_pj2)
                m_at[j] = eq.op(Y, lambda y, j=j, xu=xu: y.dma_start(
                    out=xt_stage2[:, xu, :].rearrange("p (t c) -> p t c", t=8),
                    in_=kk[:, j, :].rearrange("p (t c) -> p t c", t=8),
                    transpose=True), inc=16, key="yt%d" % xu)
            if j >= 1:
                pj = j - 1
                buf = pj % 2
                xu = pj % 2
                eq.wait(T, m_at[pj])
                eq.wait(T, m_wproj)
                eq.wait(T, m_bp)
                pja = 4 * (pj % 2) + 2
                for half, bi in ((0, pja), (1, pja + 1)):
                    t_acquire(bi)
                    for c in range(8):
                        eq.silent(T, lambda t, c=c, half=half, bi=bi, xu=xu: t.matmul(
                            pb[bi][:, :], xt_stage2[:, xu, c * 128:(c + 1) * 128],
                            wproj[:, c, hs(half)], start=(c == 0), stop=False, skip_group_check=True))
                    eq.op(T, lambda t, half=half, bi=bi: t.matmul(
                        pb[bi][:, :], onesb_at(RS1), biasrow(RS1, half),
                        start=False, stop=True, skip_group_check=True))
                m_pj2 = eq.mark(T)
                eq.wait(V, m_pj2)
                eq.wait(V, xm[pj])
                eq.wait(V, m_h32dma)
                for half, bi in ((0, pja), (1, pja + 1)):
                    eq.op(V, lambda v, half=half, bi=bi, buf=buf: v.tensor_add(
                        h32_stage[:, hs(half)], pb[bi][:, :], x_stage[:, buf, hs(half)]))
                m_r = eq.mark(V)
                bank_free[pja] = bank_free[pja + 1] = m_r
                xbuf_free[buf] = m_r
                if pj + 2 < NT:
                    xm.append(load_x_tile(b, pj + 2, buf))
                eq.wait(S, m_r)
                m_rsq = eq.op(S, lambda s: s.activation(sq, h32_stage, AF.Square,
                                                        accum_out=stats[:, C_VARS:C_VARS + 1]))
                eq.op(V, lambda v: v.reduce_sum(stats[:, C_SUM:C_SUM + 1], h32_stage, axis=AX.X))
                eq.wait(V, eq.mark("V"))
                eq.op(V, lambda v: v.tensor_scalar_mul(stats[:, C_MEAN:C_MEAN + 1], stats[:, C_SUM:C_SUM + 1], 1.0 / 1024))
                eq.wait(V, eq.mark("V"))
                eq.wait(V, m_rsq)
                eq.op(V, lambda v: v.tensor_scalar(
                    stats[:, C_LNV:C_LNV + 1], stats[:, C_MEAN:C_MEAN + 1], stats[:, C_MEAN:C_MEAN + 1], None,
                    ALU.mult))
                eq.wait(V, eq.mark("V"))
                eq.op(V, lambda v: v.tensor_scalar(
                    stats[:, C_VEPS:C_VEPS + 1], stats[:, C_VARS:C_VARS + 1], 1.0 / 1024, EPS_LN, ALU.mult, ALU.add))
                eq.wait(V, eq.mark("V"))
                m_v = eq.op(V, lambda v: v.tensor_sub(
                    stats[:, C_VEPS:C_VEPS + 1], stats[:, C_VEPS:C_VEPS + 1], stats[:, C_LNV:C_LNV + 1]))
                eq.wait(S, m_v)
                eq.op(S, lambda s: s.activation(stats[:, C_LNV:C_LNV + 1], stats[:, C_VEPS:C_VEPS + 1], AF.Ln))
                eq.wait(S, eq.mark("S"))
                m_rs = eq.op(S, lambda s: s.activation(stats[:, C_RSTD:C_RSTD + 1], stats[:, C_LNV:C_LNV + 1],
                                                       AF.Exp, scale=-0.5))
                eq.wait(V, m_rs)
                eq.op(V, lambda v: v.tensor_scalar(
                    h32_stage, h32_stage, stats[:, C_MEAN:C_MEAN + 1],
                    stats[:, C_RSTD:C_RSTD + 1], ALU.subtract, ALU.mult))
                m_h32 = eq.op(V, lambda v, b=b, pj=pj: v.tensor_copy(hh[:, b * 8 + pj, :], h32_stage))
                eq.wait(S, m_h32)
                m_h32dma = eq.op(S, lambda s, b=b, pj=pj: s.dma_start(
                    out=h32_d[b * 1024 + pj * 128:b * 1024 + (pj + 1) * 128, :], in_=h32_stage), inc=16, key="sdh")
        return m_h32dma

    h_done = []
    m_phi = None
    if not USE_EARLY_PHI:
        pass
    for b in range(2):
        m_bias = emit_bias_slots(b)
        emit_halo_proj(b, m_bias)
        m_ag1 = emit_pass1a(b, h_done[-1] if h_done else None)
        if b == 1:
            m_wcat_T_free = eq.mark(T)
        m_k2rd = emit_shift_zp2(b)
        m_sig, m_sigk2, m_qw = emit_sigmoid_batch(b, m_k2rd)
        m_carry, m_ckv = emit_carry_setup(b, m_ag1, m_sig)
        m_p1b = emit_pass1b(b, m_carry, m_ckv, m_sig, h_done[-1] if h_done else None)
        m_oarsh, m_qsh = emit_shift_oar_q(b, m_p1b)
        m_ag2 = emit_pass2a(b, m_oarsh, m_sigk2)
        if b == 1 and USE_EARLY_PHI:
            eq.wait(G, m_wcat_T_free)
            m_phi = eq.op(G, lambda g: g.dma_start(
                out=phi, in_=phi_d.rearrange("(c p) s -> p c s", p=128)), inc=16, key="gw")
        m_h = emit_pass2b(b, m_ag2, m_qsh)
        h_done.append(m_h)
    del b
    m_att_T = eq.mark(T)
    m_att_S = eq.mark(S)
    if not USE_EARLY_PHI:
        eq.wait(G, eq.mark(T))
        eq.wait(G, eq.mark(S))
        m_phi = eq.op(G, lambda g: g.dma_start(
            out=phi, in_=phi_d.rearrange("(c p) s -> p c s", p=128)), inc=16, key="gw")
    if USE_WEXPB:
        for n in range(8):
            emit_wexp_cast(n)

    if upto == "attn":
        _finalize(nc, eq, ctx, dbg, debug, h32_d=h32_d, rows=rows, d0=d0, zq=zq, kk=kk)
        return nc

    m_e2_free = [None, None]
    m_l_prev = [None, None]
    m_d0_prev = None
    eq.wait(V, eq.mark("sdh"))
    for mj in range(NT):
        m_ht = []
        eq.wait(Y, eq.mark("yt"))
        for b in range(2):
            eq.wait(Y, h_done[1])
            eq.wait(Y, m_phi)
            eq.wait(Y, m_l_prev[b])
            m_ht.append(eq.op(Y, lambda y, b=b, mj=mj: y.dma_start(
                out=ht_stage[:, b, :].rearrange("p (t c) -> p t c", t=8),
                in_=hh[:, b * 8 + mj, :].rearrange("p (t c) -> p t c", t=8),
                transpose=True), inc=16, key="yt"))
        eq.wait(T, m_phi)
        m_l = [None, None]
        m_e2 = [None, None]
        for b in range(2):
            eq.wait(T, m_ht[1])
            for ch in range(4):
                bi = 4 * b + ch
                t_acquire(bi)
                for c in range(8):
                    emitfn = eq.op if c == 7 else eq.silent
                    emitfn(T, lambda t, b=b, c=c, ch=ch, bi=bi: t.matmul(
                        pb[bi][:, :], ht_stage[:, b, c * 128:(c + 1) * 128],
                        phi[:, c, ch * H:(ch + 1) * H], start=(c == 0), stop=(c == 7),
                        skip_group_check=True))
            m_l[b] = eq.mark(T)
            eq.wait(S, m_l[b])
            eq.wait(S, m_e2_free[b])
            for ch in range(4):
                eq.op(S, lambda s, b=b, ch=ch: s.activation(
                    e2x[:, b, ch * H:(ch + 1) * H], pb[4 * b + ch][:, :], AF.Exp,
                    accum_out=stats[:, C_ESUM4 + 4 * b + ch:C_ESUM4 + 4 * b + ch + 1]))
            m_e2[b] = eq.mark(S)
            for ch in range(4):
                bank_free[4 * b + ch] = m_e2[b]
        m_l_prev = m_l
        eq.wait(V, m_e2[1])
        eq.wait(V, m_d0_prev)
        m_a0 = eq.op(V, lambda v: v.tensor_add(rcpf, e2x[:, 0, 0:1024], e2x[:, 1, 0:1024]))
        m_a1 = eq.op(V, lambda v: v.tensor_add(h32_stage, e2x[:, 0, 1024:2048], e2x[:, 1, 1024:2048]))
        eq.wait(S, m_a1)
        eq.op(S, lambda s: s.activation(rcpf, rcpf, AF.Ln))
        eq.op(S, lambda s: s.activation(rcpb[:, 0:1024], rcpf, AF.Exp, scale=-1.0))
        eq.op(S, lambda s: s.activation(h32_stage, h32_stage, AF.Ln))
        eq.op(S, lambda s: s.activation(rcpb[:, 1024:2048], h32_stage, AF.Exp, scale=-1.0))
        m_dS = eq.mark(S)
        eq.wait(V, m_dS)
        m_d0_prev = eq.op(V, lambda v, mj=mj: v.tensor_mul(d0[:, mj, :], e2x[:, 0, :], rcpb))
        for b in range(2):
            eq.wait(V, m_e2[b])
            eq.op(V, lambda v, b=b: v.reduce_sum(
                stats[:, C_ESUM:C_ESUM + 1], stats[:, C_ESUM4 + 4 * b:C_ESUM4 + 4 * b + 4], axis=AX.X))
            eq.wait(V, eq.mark("V"))
            eq.op(V, lambda v: v.reciprocal(stats[:, C_ERCP:C_ERCP + 1], stats[:, C_ESUM:C_ESUM + 1]))
            eq.wait(V, eq.mark("V"))
            m_cb = eq.op(V, lambda v, b=b: v.tensor_scalar_mul(e2x[:, b, :], e2x[:, b, :], stats[:, C_ERCP:C_ERCP + 1]))
            eq.wait(S, m_cb)
            m_comb_tp = eq.op(S, lambda s, b=b, mj=mj: s.dma_start(
                out=combT[:, :, b, mj, :],
                in_=e2x[:, b, :].rearrange("p (t c) -> p t c", t=16),
                transpose=True), inc=16, key="st")
            if b == 1:
                m_e2_free = [m_comb_tp, m_comb_tp]
    m_ffni_V = eq.mark(V)
    m_ffni_Y = eq.mark(Y)
    m_ffni_T = eq.mark(T)

    eq.wait(T, m_ffni_V)
    eq.wait(S, m_ffni_V)
    eq.wait(V, m_ffni_T)
    wexp0 = _bv(ov, OFF_FSTG, 16384, BF16).rearrange("p (c o) -> p c o", c=8)
    psred_rows = psred_d.ap().rearrange("n b p d -> (n b p) d")
    sl_dma = []
    m_slots_T_b1 = None
    m_rs = [None, None]
    m_agy = [None, None]
    m_ymm = {}
    m_ydma = {}

    slots_gate = {"b1": None}

    def emit_slots_st(st):
        nonlocal_marks = slots_gate
        if True:
            for b in range(2):
                u = b
                if b == 1:
                    eq.wait(S, nonlocal_marks.get("b1"))
                    for mj in range(NT):
                        eq.op(S, lambda s, st=st, mj=mj: s.activation(
                            d1_stage[:, mj * 128:(mj + 1) * 128], d0[:, mj, st * 128:(st + 1) * 128],
                            AF.Copy, bias=1.0, scale=-1.0))
                m_d1 = eq.mark(S)
                bi0, bi1 = 2 * b, 2 * b + 1
                t_acquire(bi0)
                t_acquire(bi1)
                if b == 1:
                    eq.wait(T, m_d1)
                for mj in range(NT):
                    emitfn = eq.op if mj == NT - 1 else eq.silent
                    for half, bi in ((0, bi0), (1, bi1)):
                        if b == 0:
                            emitfn(T, lambda t, st=st, mj=mj, half=half, bi=bi: t.matmul(
                                pb[bi][:, :], d0[:, mj, st * 128:(st + 1) * 128],
                                hh[:, mj, hs(half)],
                                start=(mj == 0), stop=(mj == NT - 1), skip_group_check=True))
                        else:
                            emitfn(T, lambda t, st=st, mj=mj, half=half, bi=bi: t.matmul(
                                pb[bi][:, :], d1_stage[:, mj * 128:(mj + 1) * 128],
                                hh[:, 8 + mj, hs(half)],
                                start=(mj == 0), stop=(mj == NT - 1), skip_group_check=True))
                m_mm = eq.mark(T)
                if b == 1:
                    nonlocal_marks["b1"] = m_mm
                eq.wait(V, m_mm)
                if len(sl_dma) >= 2:
                    eq.wait(V, sl_dma[-2])
                for half, bi in ((0, bi0), (1, bi1)):
                    eq.op(V, lambda v, u=u, half=half, bi=bi: v.tensor_copy(sl_stage[:, u, hs(half)], pb[bi][:, :]))
                m_ev = eq.mark(V)
                bank_free[bi0] = bank_free[bi1] = m_ev
                eq.wait(G, m_ev)
                sl_dma.append(eq.op(G, lambda g, st=st, b=b, u=u: g.dma_start(
                    out=ps_d.ap()[4 * st:4 * (st + 1), b, :, :], in_=sl_stage[:, u, :]),
                    inc=16, key="gp%d" % u))

    def emit_rs(ch):
        eq.wait(G, eq.mark("gp0"))
        eq.wait(G, eq.mark("gp1"))
        m_rs[ch] = eq.op(G, lambda g, ch=ch: g.collective_compute(
            "ReduceScatter", ALU.add, replica_groups=[list(range(NCORES))],
            ins=[ps_d.ap()[32 * ch:32 * (ch + 1), :, :, :].opt()],
            outs=[psred_d.ap()[4 * ch:4 * (ch + 1), :, :, :].opt()]), key="gcc")

    m_sct = {}
    m_scb = {}

    def emit_y_group(ch, gi, wbuf, m_wgate):
        if True:
            grp = 2 * ch + gi
            ug = grp % 2
            if gi == 0:
                eq.wait(G, m_rs[ch])
            if grp >= 2:
                eq.wait(G, m_scb[grp - 2])
            m_scr = eq.op(G, lambda g, grp=grp, ug=ug: g.dma_start(
                out=sc_stage[:, ug, :], in_=psred_rows[grp * 128:(grp + 1) * 128, :]),
                inc=16, key="gr%d" % ug)
            eq.wait(S, m_scr)
            if grp >= 2:
                eq.wait(S, m_sct.get(grp - 2) or m_sct_prev[grp - 2])
            m_scb[grp] = eq.op(S, lambda s, ug=ug: s.copy(scb_stage[:, ug, :], sc_stage[:, ug, :]))
            eq.wait(Y, m_scb[grp])
            if grp >= 2:
                eq.wait(Y, m_ymm.get(2 * (grp - 2) + 1))
            m_sct[grp] = eq.op(Y, lambda y, ug=ug: y.dma_start(
                out=sct_stage[:, ug, :].rearrange("p (t c) -> p t c", t=8),
                in_=scb_stage[:, ug, :].rearrange("p (t c) -> p t c", t=8),
                transpose=True), inc=16, key="yt")
            for k in (0, 1):
                n = 2 * grp + k
                nk = 2 * gi + k
                wv, wkey, widx = wbuf[n % len(wbuf)]
                WE = (Y if n % 2 == 0 else G) if USE_WEXPB else G
                eq.wait(WE, m_wgate)
                eq.wait(WE, eq.mark("gwc"))
                if ch == 0 and n >= 1:
                    eq.wait(WE, m_ymm[n - 1])
                if ch == 1 and n - 4 >= len(wbuf):
                    eq.wait(WE, m_ymm[n - len(wbuf)])
                if USE_WEXPB:
                    m_wx = eq.op(WE, lambda g, n=n, wv=wv: g.dma_start(
                        out=wv, in_=wexpb_d[n].rearrange("(c p) o -> p c o", p=128)),
                        inc=16, key=wkey + ("y" if n % 2 == 0 else "g"))
                else:
                    m_wx = eq.op(G, lambda g, n=n, wv=wv: g.dma_start(
                        out=wv, in_=wexp_d[n].rearrange("(c p) o -> p c o", p=128)),
                        inc=16, key=wkey)
                bbase, bh = (RS0, RS1, RS2, RS0)[nk], (0, 0, 0, 1)[nk]
                eq.wait(T, m_bex[ch])
                eq.wait(T, m_sct[grp])
                eq.wait(T, m_wx)
                t_acquire(4)
                t_acquire(5)
                colb = k * 64
                for half, bi in ((0, 4), (1, 5)):
                    for c in range(8):
                        eq.silent(T, lambda t, wv=wv, c=c, half=half, bi=bi, ug=ug, colb=colb: t.matmul(
                            pb[bi][0:64, :], sct_stage[:, ug, c * 128 + colb:c * 128 + colb + 64],
                            wv[:, c, hs(half)], start=(c == 0), stop=False, skip_group_check=True))
                    eq.op(T, lambda t, half=half, bi=bi, bbase=bbase, bh=bh: t.matmul(
                        pb[bi][0:64, :], onesb_at(bbase, 64),
                        rowsb[bbase:bbase + 1, bh * 1024 + half * H:bh * 1024 + (half + 1) * H],
                        start=False, stop=True, skip_group_check=True))
                m_ymm[n] = eq.mark(T)
                eq.wait(V, m_ymm[n])
                un = n % 2
                if n >= 2:
                    eq.wait(V, m_ydma[n - 2])
                for half, bi in ((0, 4), (1, 5)):
                    eq.op(V, lambda v, un=un, half=half, bi=bi: v.tensor_copy(
                        y_stage[0:64, un, hs(half)], pb[bi][0:64, :]))
                m_ye = eq.mark(V)
                bank_free[4] = bank_free[5] = m_ye
                eq.wait(G, m_ye)
                m_ydma[n] = eq.op(G, lambda g, n=n, un=un: g.dma_start(
                    out=yin_d.ap()[n * 64:(n + 1) * 64, :], in_=y_stage[0:64, un, :]),
                    inc=16, key="gy%d" % un)
    def emit_agy(ch):
        eq.wait(G, eq.mark("gy0"))
        eq.wait(G, eq.mark("gy1"))
        m_agy[ch] = eq.op(G, lambda g, ch=ch: g.collective_compute(
            "AllGather", ALU.bypass, replica_groups=[list(range(NCORES))],
            ins=[yin_d.ap()[256 * ch:256 * (ch + 1), :].opt()],
            outs=[yag_d.ap()[2048 * ch:2048 * (ch + 1), :].opt()]), key="gcc")

    for st in range(8):
        emit_slots_st(st)
    emit_rs(0)
    eq.wait(G, m_ffni_T)
    eq.wait(G, m_ffni_V)
    eq.wait(G, eq.mark("yt"))
    eq.wait(G, eq.mark("st"))
    wbuf0 = [(wexp0, "gx0", None)]
    m_bex = [None, None]

    def emit_bexp_rows(ch, gate):
        eq.wait(G, gate)
        mm = None
        for nk in range(4):
            bbase, bh = (RS0, RS1, RS2, RS0)[nk], (0, 0, 0, 1)[nk]
            mm = eq.op(G, lambda g, ch=ch, nk=nk, bbase=bbase, bh=bh: g.dma_start(
                out=rowsb[bbase:bbase + 1, bh * 1024:(bh + 1) * 1024],
                in_=brows_d[B_BEXP + 4 * ch + nk:B_BEXP + 4 * ch + nk + 1, :]), inc=16, key="grow")
        m_bex[ch] = mm

    emit_bexp_rows(0, m_ffni_V)
    for st in range(8, 16):
        emit_slots_st(st)
        if st == 11:
            emit_y_group(0, 0, wbuf0, m_ffni_V)
        if st == 15:
            emit_y_group(0, 1, wbuf0, m_ffni_V)
    emit_rs(1)
    emit_agy(0)
    m_slots_T = eq.mark(T)
    eq.wait(G, m_slots_T)
    wbuf1 = [(wexp[:, 0, :, :], "gx0", None), (wexp[:, 1, :, :], "gx1", None),
             (_bv(ov, OFF_H, 16384, BF16).rearrange("p (c o) -> p c o", c=8), "gx2", None),
             (_bv(ov, OFF_H + 16384, 16384, BF16).rearrange("p (c o) -> p c o", c=8), "gx3", None)]
    eq.wait(G, eq.mark(T))
    emit_bexp_rows(1, m_slots_T)
    emit_y_group(1, 0, wbuf1, m_slots_T)
    emit_y_group(1, 1, wbuf1, m_slots_T)
    emit_agy(1)

    m_yld = {}
    for ch in range(2):
        yag_r = yag_d.ap()[2048 * ch:2048 * (ch + 1), :].rearrange("(n b p) d -> n b p d", n=32, b=2)
        eq.wait(G, m_agy[ch])
        if ch == 1:
            eq.wait(Y, m_agy[ch])
        for st in range(8 * ch, 8 * ch + 8):
            for b in range(2):
                e_, k_ = ((G, "gyy") if (ch == 0 or (2 * st + b) % 2 == 0) else (Y, "gyyY"))
                m_yld[(st, b)] = eq.op(e_, lambda g, st=st, b=b, ch=ch, yag_r=yag_r: g.dma_start(
                    out=yy[:, st, b, :], in_=yag_r[4 * (st - 8 * ch):4 * (st - 8 * ch + 1), b, :, :]),
                    inc=16, key=k_)
    m_outdma = None
    m_fin_prev = None
    m_hld_prev = None
    for b in range(2):
        for mj in range(NT):
            eq.wait(S, m_fin_prev)
            m_hld = eq.op(S, lambda s, b=b, mj=mj: s.dma_start(
                out=h32_stage, in_=h32_d[b * 1024 + mj * 128:b * 1024 + (mj + 1) * 128, :]), inc=16, key="sdh")
            oa = 2 * ((b * NT + mj) % 2)
            t_acquire(oa)
            t_acquire(oa + 1)
            eq.wait(T, eq.mark("gyy"))
            eq.wait(T, eq.mark("gyyY"))
            eq.wait(T, eq.mark("yt"))
            eq.wait(T, eq.mark("st"))
            for st in range(16):
                emitfn = eq.op if st == 15 else eq.silent
                for half, bi in ((0, oa), (1, oa + 1)):
                    emitfn(T, lambda t, st=st, b=b, mj=mj, half=half, bi=bi: t.matmul(
                        pb[bi][:, :], combT[:, st, b, mj, :], yy[:, st, b, hs(half)],
                        start=(st == 0), stop=(st == 15), skip_group_check=True))
            m_mm = eq.mark(T)
            eq.wait(V, m_mm)
            eq.wait(V, m_hld)
            for half, bi in ((0, oa), (1, oa + 1)):
                eq.op(V, lambda v, half=half, bi=bi: v.tensor_add(
                    sq[:, hs(half)], pb[bi][:, :], h32_stage[:, hs(half)]))
            m_r2 = eq.mark(V)
            bank_free[oa] = bank_free[oa + 1] = m_r2
            eq.wait(S, m_r2)
            m_r2s = eq.op(S, lambda s: s.activation(h32_stage, sq, AF.Square,
                                                    accum_out=stats[:, C_VARS:C_VARS + 1]))
            eq.op(V, lambda v: v.reduce_sum(stats[:, C_SUM:C_SUM + 1], sq, axis=AX.X))
            eq.wait(V, eq.mark("V"))
            eq.op(V, lambda v: v.tensor_scalar_mul(stats[:, C_MEAN:C_MEAN + 1], stats[:, C_SUM:C_SUM + 1], 1.0 / 1024))
            eq.wait(V, eq.mark("V"))
            eq.wait(V, m_r2s)
            eq.op(V, lambda v: v.tensor_scalar(
                stats[:, C_LNV:C_LNV + 1], stats[:, C_MEAN:C_MEAN + 1], stats[:, C_MEAN:C_MEAN + 1], None,
                ALU.mult))
            eq.wait(V, eq.mark("V"))
            eq.op(V, lambda v: v.tensor_scalar(
                stats[:, C_VEPS:C_VEPS + 1], stats[:, C_VARS:C_VARS + 1], 1.0 / 1024, EPS_LN, ALU.mult, ALU.add))
            eq.wait(V, eq.mark("V"))
            m_v = eq.op(V, lambda v: v.tensor_sub(
                stats[:, C_VEPS:C_VEPS + 1], stats[:, C_VEPS:C_VEPS + 1], stats[:, C_LNV:C_LNV + 1]))
            eq.wait(S, m_v)
            eq.op(S, lambda s: s.activation(stats[:, C_LNV:C_LNV + 1], stats[:, C_VEPS:C_VEPS + 1], AF.Ln))
            eq.wait(S, eq.mark("S"))
            m_rstd = eq.op(S, lambda s: s.activation(stats[:, C_RSTD:C_RSTD + 1], stats[:, C_LNV:C_LNV + 1],
                                                     AF.Exp, scale=-0.5))
            eq.wait(V, m_rstd)
            eq.wait(V, m_outdma)
            pbuf = (b * NT + mj) % 2
            m_fin = eq.op(V, lambda v, pbuf=pbuf: v.tensor_scalar(
                x_stage[:, pbuf, :], sq, stats[:, C_MEAN:C_MEAN + 1],
                stats[:, C_RSTD:C_RSTD + 1], ALU.subtract, ALU.mult))
            m_fin_prev = m_fin
            eq.wait(S, m_fin)
            m_outdma = eq.op(S, lambda s, b=b, mj=mj, pbuf=pbuf: s.dma_start(
                out=out_d[b, mj * 128:(mj + 1) * 128, :], in_=x_stage[:, pbuf, :]), inc=16, key="sdo")

    _finalize(nc, eq, ctx, dbg, debug, h32_d=h32_d, rows=rows, d0=d0, zq=zq, kk=kk)
    return nc


def _finalize(nc, eq, ctx, dbg, debug, h32_d=None, rows=None, d0=None, zq=None, kk=None):
    G = "G"
    if debug:
        for k in list(eq.cnt):
            eq.wait(G, eq.mark(k))
        if "dbg_h" in dbg and h32_d is not None:
            eq.op(G, lambda g: g.dma_start(out=dbg["dbg_h"], in_=h32_d), inc=16, key="gdbg")
        if "dbg_rows" in dbg and rows is not None:
            eq.op(G, lambda g: g.dma_start(out=dbg["dbg_rows"][0:16, :], in_=rows[0:16, :]), inc=16, key="gdbg")
        if "dbg_d0" in dbg and d0 is not None:
            eq.op(G, lambda g: g.dma_start(
                out=dbg["dbg_d0"].rearrange("(j p) s -> p j s", p=128), in_=d0), inc=16, key="gdbg")
        if "dbg_q" in dbg and zq is not None:
            eq.op(G, lambda g: g.dma_start(
                out=dbg["dbg_q"][:, 0:1024].rearrange("(j p) s -> p j s", p=128), in_=zq), inc=16, key="gdbg")
        if "dbg_oar" in dbg and kk is not None:
            eq.op(G, lambda g: g.dma_start(
                out=dbg["dbg_oar"][:, 0:1024].rearrange("(j p) s -> p j s", p=128), in_=kk), inc=16, key="gdbg")
    finals = {k: eq.mark(k) for k in list(eq.cnt)}
    for e in eq.ENGS:
        for k, mk in finals.items():
            if k != e:
                eq.wait(e, mk)

    block = ctx.enter_context(nc.Block())

    @block.tensor
    def _(t):
        for fn in eq.q["T"]:
            fn(t)

    @block.vector
    def _(v):
        for fn in eq.q["V"]:
            fn(v)

    @block.scalar
    def _(s):
        for fn in eq.q["S"]:
            fn(s)

    @block.gpsimd
    def _(g):
        for fn in eq.q["G"]:
            fn(g)

    @block.sync
    def _(y):
        for fn in eq.q["Y"]:
            fn(y)

    ctx.close()


def _host_inputs(inputs):
    x = np.ascontiguousarray(np.asarray(inputs["L0_attn_x"], dtype=np.float32))
    Wq1 = np.asarray(inputs["L0_attn_Wq1"], np.float32)
    Wk1 = np.asarray(inputs["L0_attn_Wk1"], np.float32)
    Wk2 = np.asarray(inputs["L0_attn_Wk2"], np.float32)
    Wproj = np.asarray(inputs["L0_attn_Wproj"], np.float32)
    bq1 = np.asarray(inputs["L0_attn_bq1"], np.float32)
    bk1 = np.asarray(inputs["L0_attn_bk1"], np.float32)
    bk2 = np.asarray(inputs["L0_attn_bk2"], np.float32)
    bproj = np.asarray(inputs["L0_attn_bproj"], np.float32)
    phi = np.ascontiguousarray(np.asarray(inputs["L0_ffn_phi"], np.float32).reshape(1024, 2048))
    W = np.asarray(inputs["L0_ffn_W"], np.float32)
    bexp = np.asarray(inputs["L0_ffn_b"], np.float32)

    wcat = np.ascontiguousarray(np.concatenate([Wq1.T, Wk1.T, Wk2.T], axis=1))
    wproj = np.ascontiguousarray(Wproj.T)
    tri = np.ascontiguousarray(np.concatenate(
        [np.triu(np.ones((128, 128), np.float32)), np.ones((128, 128), np.float32)], axis=1))

    in_maps = []
    for c in range(NCORES):
        lo = c * 1024
        halo = np.zeros((2, 1, 1024), np.float32) if c == 0 else x[:, lo - 1:lo, :]
        x_pad = np.ascontiguousarray(np.concatenate([halo, x[:, lo:lo + 1024, :]], axis=1))
        brows = np.zeros((12, 1024), np.float32)
        brows[B_BQ1] = bq1
        brows[B_BK1] = bk1
        brows[B_BK2] = bk2
        brows[B_BPROJ] = bproj
        eorder = list(range(4 * c, 4 * c + 4)) + list(range(32 + 4 * c, 32 + 4 * c + 4))
        brows[B_BEXP:B_BEXP + 8] = bexp[eorder]
        pmask = np.zeros((96, 132), np.float32)
        for r in range(8):
            if r < c:
                pmask[2 * r, 0] = 1.0
                pmask[2 * r + 1, 1] = 1.0
                pmask[r, 2] = 1.0
        pmask[0, 3] = 0.0 if c == 0 else 1.0
        pmask[:, 4:132] = 1.0
        in_maps.append({
            "x": x_pad, "wcat": wcat, "wproj": wproj, "phi": phi,
            "wexp": np.ascontiguousarray(W[eorder]),
            "brows": brows, "tri": tri, "pmask": pmask,
        })
    return in_maps


_NC_CACHE = {}


def kernel(**inputs):
    in_maps = _host_inputs(inputs)
    if "full" not in _NC_CACHE:
        _NC_CACHE["full"] = build_nc()
    nc = _NC_CACHE["full"]
    res = run_bass_kernel_spmd(nc, in_maps, core_ids=list(range(NCORES)))
    out = np.empty((2, 8192, 1024), np.float32)
    for c in range(NCORES):
        out[:, c * 1024:(c + 1) * 1024, :] = res.results[c]["out"]
    return out
